# version 10
# baseline (speedup 1.0000x reference)
import numpy as np
import concourse.bass as bass
import concourse.mybir as mybir
from concourse.bass_utils import run_bass_kernel_spmd
from contextlib import ExitStack

F32 = mybir.dt.float32
BF16 = mybir.dt.bfloat16
AF = mybir.ActivationFunctionType
ALU = mybir.AluOpType
PI = float(np.pi)

NCORES = 8
D = 2048
KC = 16
TPC = 2048
HALO = 128
CH = 512
NCH = TPC // CH
JSL = 1024
FOLD_LROWS = 2049
INVN = 1.0 / 2048.0

ENGINES = ("sync", "gpsimd", "vector", "scalar", "tensor")


class Sched:
    def __init__(self, sems):
        self.q = {e: [] for e in ENGINES}
        self.sems = sems
        self.cnt = {name: 0 for name in sems}
        self.hw = {e: {name: 0 for name in sems} for e in ENGINES}

    ENG_SEM = {"vector": "s_dve", "scalar": "s_act", "gpsimd": "s_pool"}

    def op(self, eng, fn, sem=None, amt=1, is_dma=False):
        es = None if is_dma else self.ENG_SEM.get(eng)
        if es is not None and sem != "s_cc":
            self.wait(eng, (es, self.cnt[es]))
            self.cnt[es] += 1
            h = self.sems[es]
            self.q[eng].append(lambda e, fn=fn, h=h: fn(e).then_inc(h, 1))
            return (es, self.cnt[es])
        if sem is None:
            self.q[eng].append(fn)
            return None
        if es is not None and sem == "s_cc":
            self.wait(eng, (es, self.cnt[es]))
        self.cnt[sem] += amt
        h = self.sems[sem]
        self.q[eng].append(lambda e, fn=fn, h=h, a=amt: fn(e).then_inc(h, a))
        return (sem, self.cnt[sem])

    def mark(self, sem):
        return (sem, self.cnt[sem])

    def wait(self, eng, ms):
        if ms is None:
            return
        sem, val = ms
        if self.hw[eng][sem] >= val:
            return
        self.hw[eng][sem] = val
        h = self.sems[sem]
        self.q[eng].append(lambda e, h=h, v=val: e.wait_ge(h, v))

    def waits(self, eng, mss):
        best = {}
        for ms in mss:
            if ms is None:
                continue
            sem, val = ms
            if best.get(sem, -1) < val:
                best[sem] = val
        for sem, val in best.items():
            self.wait(eng, (sem, val))

    def dma(self, eng, out_ap, in_ap, sem, **kw):
        return self.op(eng, lambda e: e.dma_start(out_ap, in_ap, **kw), sem, 16,
                       is_dma=True)


def build(taps=()):
    taps = set(taps)
    nc = bass.Bass("TRN2", target_bir_lowering=False)

    def inp(name, shape, dtype=F32):
        return nc.dram_tensor(name, shape, dtype, kind="ExternalInput")

    x_fm = inp("x_fm", [D, HALO + TPC])
    w_qkvT = inp("w_qkvT", [D, 3 * D])
    w_oT = inp("w_oT", [D, D])
    w_amp = inp("w_amp", [D, D])
    w_ph = inp("w_ph", [D, D])
    entwT_sl = inp("entwT_sl", [2, JSL, D])
    entpw_sl = inp("entpw_sl", [2, JSL, D])
    entb_col = inp("entb_col", [2, 128, JSL // 128])
    entpb_col = inp("entpb_col", [2, 128, 16])
    pathw_sl = inp("pathw_sl", [2, 2, 256, D])
    pathb_col = inp("pathb_col", [2, 2, 128, 16])
    phase_col = inp("phase_col", [2, 2, 128, 16])
    ampb_col = inp("ampb_col", [128, 16])
    phb_col = inp("phb_col", [128, 16])
    ln3_gcol = inp("ln3_gcol", [2, 128, 16])
    ln4_gcol = inp("ln4_gcol", [2, 128, 16])
    ln3_bcol = inp("ln3_bcol", [2, 128, 16])
    ln4_bcol = inp("ln4_bcol", [2, 128, 16])
    masks_in = inp("masks", [3, 128, 128])

    out_fm = nc.dram_tensor("out_fm", [D, TPC], F32, kind="ExternalOutput")
    wbf = nc.dram_tensor("wbf", [D, 6 * D], BF16, kind="Internal")
    r1s = nc.dram_tensor("r1s", [D, TPC], F32, kind="Internal")
    fold_in = nc.dram_tensor("fold_in", [2 * FOLD_LROWS, D], BF16, kind="Internal")
    fold_out = nc.dram_tensor("fold_out", [2 * FOLD_LROWS, D], BF16, kind="Internal")
    modrow_d = nc.dram_tensor("modrow_d", [4, D], F32, kind="Internal")
    pg_in = nc.dram_tensor("pg_in", [512, D], BF16, kind="Internal")
    pg_out = nc.dram_tensor("pg_out", [NCORES * 512, D], BF16, kind="Internal")

    tap_t = {}
    tapdefs = {
        "lnx0": ([128, 10240], BF16), "r1": ([D, TPC], F32),
        "mfold": ([2 * FOLD_LROWS, D], BF16), "pg": ([NCORES * 512, D], BF16),
        "h0": ([128, 8192], F32), "pre3": ([128, 8192], F32),
        "h1": ([128, 8192], F32), "araw0": ([128, 8192], BF16),
        "qbf0": ([128, 8192], BF16), "kbf0": ([128, 10240], BF16),
        "vbf0": ([128, 10240], BF16), "lnr10": ([128, 10240], BF16),
        "modrow": ([4, D], F32), "pgin": ([512, D], BF16), "colm": ([128, 64], F32),
        "pre4": ([128, 8192], F32), "h2": ([128, 8192], F32),
        "pre3b": ([128, 8192], F32), "h4": ([128, 8192], F32),
        "qbfa": ([128, 8192], BF16), "qbfb": ([128, 8192], BF16),
        "ptld": ([128, 32], F32), "btot": ([128, 32], F32),
    }
    for t in taps:
        shape, dt = tapdefs[t]
        tap_t[t] = nc.dram_tensor("tap_" + t, shape, dt, kind="ExternalOutput")

    with ExitStack() as st:
        NP = 49920
        pool = st.enter_context(nc.sbuf_tensor("pool", [128, NP], F32))
        poolF = pool.ap()
        poolH = pool.ap().bitcast(BF16)

        def fview(off, width, p0=0, pn=128):
            return poolF[p0:p0 + pn, off:off + width]

        def hview(off_u, width_bf, p0=0, pn=128):
            return poolH[p0:p0 + pn, 2 * off_u: 2 * off_u + width_bf]

        O_XR, O_LNBF, O_QBF, O_SLAB = 0, 10240, 15360, 19456
        O_TS, O_ROWS, O_ATT, O_U = 27648, 30720, 32768, 33536

        XR = [fview(O_XR + i * 640, 640) for i in range(KC)]
        LNBF = [hview(O_LNBF + i * 320, 640) for i in range(KC)]
        QBF = [hview(O_QBF + i * 256, 512) for i in range(KC)]
        TS = [fview(O_TS + i * 512, 512) for i in range(6)]
        TS0BF = hview(O_TS, 512)
        S1 = fview(O_ATT, 256)
        S2 = fview(O_ATT + 256, 256)
        ATT = [hview(O_ATT + 512 + i * 128, 256) for i in range(2)]
        rows_a = fview(O_ROWS, 512, 0, 1)
        rows_b = fview(O_ROWS + 512, 512, 0, 1)
        rows_rstd = fview(O_ROWS + 1024, 512, 0, 1)
        rows2_bf = hview(O_ROWS + 1536, 512, 0, 2)
        onesf_row = fview(O_ROWS + 1792, 128, 0, 1)
        onesb_row = hview(O_ROWS + 1920, 128, 0, 1)
        U = O_U
        FW_T = [hview(U + j * 1024, 2048) for j in range(8)]
        PWB = [U + 8192, U + 8192 + 2048]
        FSTG = [hview(U + 12288 + j * 256, 512) for j in range(4)]
        BSTG = hview(U + 13312, 2048, 0, 1)
        PW0 = fview(U + 0, 2048)
        PW1 = fview(U + 2048, 2048)
        MODROW = [fview(U + 8192 + i * 2048, 2048, 0, 1) for i in range(2)]
        PSTG = [hview(U + 12288 + j * 1024, 2048) for j in range(2)]
        KBF = [hview(U + i * 320, 640) for i in range(KC)]
        VBF = [hview(U + 5120 + i * 1024, 2048) for i in range(5)]
        ARAW = [hview(U + 10240 + i * 256, 512) for i in range(KC)]
        HA = [fview(U + i * 512, 512) for i in range(KC)]
        HB = [fview(U + 8192 + i * 512, 512) for i in range(KC)]

        def smallt(name, shape, dtype=F32):
            return st.enter_context(nc.sbuf_tensor(name, shape, dtype))

        masks = smallt("masks_sb", [128, 3 * 128])
        colp = smallt("colp", [128, 64])
        colm = smallt("colm", [128, 64])
        colw = smallt("colw", [128, 64])
        col_ampb = smallt("col_ampb", [128, 16])
        col_phb = smallt("col_phb", [128, 16])
        col_entb = smallt("col_entb", [128, 16], BF16)
        col_entpb = smallt("col_entpb", [128, 32])
        col_btot = smallt("col_btot", [128, 32])
        col_ptld = smallt("col_ptld", [128, 32])
        col_pb = smallt("col_pb", [128, 64])
        col_g3 = smallt("col_g3", [128, 32])
        col_g4 = smallt("col_g4", [128, 32])
        col_b3 = smallt("col_b3", [128, 32])
        col_b4 = smallt("col_b4", [128, 32])
        ones_col_bf = smallt("ones_col_bf", [128, 1], BF16)
        pihalf_col = smallt("pihalf_col", [128, 1])
        eps1 = smallt("eps1", [1, 1])

        def mview(i):
            return masks.ap()[:, i * 128:(i + 1) * 128]

        PM = [st.enter_context(nc.psum_tensor(f"pm{i}", [128, 512], F32)) for i in range(4)]
        PA = [st.enter_context(nc.psum_tensor(f"pa{i}", [128, 512], F32)) for i in range(2)]
        PS = st.enter_context(nc.psum_tensor("ps", [128, 512], F32))
        PF = st.enter_context(nc.psum_tensor("pf", [128, 512], F32))

        semnames = ["s_pe", "s_act", "s_dve", "s_pool", "s_cc",
                    "d_sm", "d_sme", "d_mw", "d_fw", "d_pw0", "d_pw1",
                    "d_fst0", "d_fst1", "d_fst2", "d_fst3", "d_bst", "d_pf",
                    "d_x", "d_st", "d_tap", "d_sl0", "d_sl1", "d_sl0s", "d_sl1s",
                    "d_gb", "d_bt", "d_wc0", "d_wc1", "d_wc2", "d_wc3"]
        sems = {n: st.enter_context(nc.semaphore(n)) for n in semnames}
        S = Sched(sems)

        bank_free = {}
        slab_cons = {0: None, 1: None}
        ln_guard = {"p1": {0: None, 1: None}, "sq": None, "ts1": None, "ts2": None}

        def mm(ps_ap, lhsT_ap, rhs_ap, start, stop, sem=None):
            return S.op("tensor",
                        lambda e, o=ps_ap, l=lhsT_ap, r=rhs_ap, a=start, b=stop:
                        e.matmul(o, l, r, start=a, stop=b), sem=sem)

        def slab_load(slot, src2d, queue=None, n_kc=16, fwidth=512):
            queue = "sync" if slot == 0 else "scalar"
            S.wait(queue, slab_cons[slot])
            src3 = src2d.rearrange("(kc p) f -> p kc f", p=128)
            dst3 = poolH[:, 2 * (O_SLAB + slot * 4096):
                         2 * (O_SLAB + slot * 4096) + n_kc * fwidth] \
                .rearrange("p (kc f) -> p kc f", f=fwidth)
            sem = f"d_sl{slot}s"
            h = n_kc // 2
            S.dma(queue, dst3[:, 0:h], src3[:, 0:h], sem)
            return S.dma(queue, dst3[:, h:n_kc], src3[:, h:n_kc], sem)

        def slab_tile(slot, kc, f0, fw, fwidth=512):
            base = 2 * (O_SLAB + slot * 4096) + kc * fwidth + f0
            return poolH[:, base:base + fw]

        def tap_sbuf(name, region_ap, mss):
            if name not in tap_t:
                return None
            S.waits("sync", mss)
            return S.dma("sync", tap_t[name].ap(), region_ap, "d_tap")

        def tap_dram(name, src_ap, mss):
            if name in tap_t:
                S.waits("sync", mss)
                S.dma("sync", tap_t[name].ap(), src_ap, "d_tap")

        def emit_ln(X, bf_tgt, fp_tgt, ttiles, gcol, bcol, pre_ms=(), pre_fc=None):
            last_ms = None
            fc_ms = [None] * KC
            for (c0, w) in ttiles:
                cast_fc = [None] * KC
                for fc in range(KC):
                    eng = "gpsimd" if fc % 2 == 0 else "vector"
                    S.waits(eng, pre_ms)
                    if pre_fc is not None:
                        S.wait(eng, pre_fc[fc])
                    dsta, srca = bf_tgt[fc][:, c0:c0 + w], X[fc][:, c0:c0 + w]
                    cast_fc[fc] = S.op(
                        eng, lambda e, d=dsta, s=srca: e.tensor_copy(d, s))
                S.wait("tensor", bank_free.get("ps"))
                S.wait("tensor", bank_free.get("pf"))
                S.wait("tensor", m_sets)
                S.wait("scalar", m_sets)
                prev_sq = ln_guard["sq"]
                for fc in range(KC):
                    S.waits("scalar", pre_ms)
                    if pre_fc is not None:
                        S.wait("scalar", pre_fc[fc])
                    S.wait("scalar", prev_sq)
                    dsta, srca = TS0BF[:, 0:w], X[fc][:, c0:c0 + w]
                    sq_ms = S.op("scalar",
                                 lambda e, d=dsta, s=srca: e.activation(d, s, AF.Square),
                                 sem="s_act")
                    S.wait("tensor", cast_fc[fc])
                    S.wait("tensor", sq_ms)
                    mm(PS.ap()[0:1, 0:w], ones_col_bf.ap(), bf_tgt[fc][:, c0:c0 + w],
                       start=(fc == 0), stop=(fc == KC - 1))
                    prev_sq = mm(PF.ap()[0:1, 0:w], ones_col_bf.ap(), TS0BF[:, 0:w],
                                 start=(fc == 0), stop=(fc == KC - 1),
                                 sem="s_pe")
                stats_pe = prev_sq
                ln_guard["sq"] = prev_sq
                S.wait("scalar", stats_pe)
                S.op("scalar", lambda e, w=w: e.activation(
                    rows_a[:, 0:w], PS.ap()[0:1, 0:w], AF.Square, scale=INVN))
                m_t1 = S.op("scalar", lambda e, w=w: e.activation(
                    rows_b[:, 0:w], PF.ap()[0:1, 0:w], AF.Copy, scale=INVN),
                    sem="s_act")
                S.wait("vector", m_t1)
                m_ve = S.op("vector", lambda e, w=w: e.tensor_tensor(
                    rows_b[:, 0:w], rows_b[:, 0:w], rows_a[:, 0:w], ALU.subtract),
                    sem="s_dve")
                S.wait("scalar", m_ve)
                S.op("scalar", lambda e, w=w: e.activation(
                    rows_a[:, 0:w], rows_b[:, 0:w], AF.Ln, bias=eps1.ap()))
                m_rstd = S.op("scalar", lambda e, w=w: e.activation(
                    rows_rstd[:, 0:w], rows_a[:, 0:w], AF.Exp, scale=-0.5), sem="s_act")
                m_mrow = S.op("scalar", lambda e, w=w: e.activation(
                    rows_a[:, 0:w], PS.ap()[0:1, 0:w], AF.Copy, scale=INVN),
                    sem="s_act")
                S.wait("tensor", m_rstd)
                m_bc = mm(PF.ap()[:, 0:w], onesf_row, rows_rstd[:, 0:w],
                          start=True, stop=True, sem="s_pe")
                S.wait("tensor", m_mrow)
                m_bc2 = mm(PS.ap()[:, 0:w], onesf_row, rows_a[:, 0:w],
                           start=True, stop=True, sem="s_pe")
                S.wait("scalar", m_bc)
                S.wait("scalar", ln_guard["ts1"])
                m_bce = S.op("scalar", lambda e, w=w: e.activation(
                    TS[1][:, 0:w], PF.ap()[:, 0:w], AF.Copy), sem="s_act")
                bank_free["pf"] = m_bce
                S.wait("scalar", m_bc2)
                S.wait("scalar", ln_guard["ts2"])
                m_bce2 = S.op("scalar", lambda e, w=w: e.activation(
                    TS[2][:, 0:w], PS.ap()[:, 0:w], AF.Copy), sem="s_act")
                bank_free["ps"] = m_bce2
                affine = gcol is not None
                for fc in range(KC):
                    S.wait("gpsimd", m_bce2)
                    S.wait("gpsimd", ln_guard["p1"][fc % 2])
                    t1d, t1s = TS[3 + fc % 2][:, 0:w], X[fc][:, c0:c0 + w]
                    m_p1 = S.op("gpsimd", lambda e, d=t1d, s=t1s, w=w:
                                e.tensor_tensor(d, s, TS[2][:, 0:w], ALU.subtract),
                                sem="s_pool")
                    ln_guard["ts2"] = m_p1
                    tgt = fp_tgt[fc][:, 0:w] if fp_tgt is not None \
                        else bf_tgt[fc][:, c0:c0 + w]
                    S.wait("vector", m_p1)
                    S.wait("vector", m_bce)
                    if affine:
                        m_t2 = S.op("vector", lambda e, d=t1d, w=w:
                                    e.tensor_tensor(d, d, TS[1][:, 0:w], ALU.mult),
                                    sem="s_dve")
                        ln_guard["ts1"] = m_t2
                        gsc = gcol[:, fc:fc + 1]
                        bsc = bcol[:, fc:fc + 1]
                        m_out = S.op("vector", lambda e, t=tgt, d=t1d, g=gsc, b=bsc:
                                     e.tensor_scalar(t, d, g, b, ALU.mult, ALU.add),
                                     sem="s_dve")
                    else:
                        m_out = S.op("vector", lambda e, t=tgt, d=t1d, w=w:
                                     e.tensor_tensor(t, d, TS[1][:, 0:w], ALU.mult),
                                     sem="s_dve")
                        ln_guard["ts1"] = m_out
                    ln_guard["p1"][fc % 2] = m_out
                    last_ms = m_out
                    if fp_tgt is not None and bf_tgt is not None:
                        S.wait("scalar", m_out)
                        cd, cs2 = bf_tgt[fc][:, c0:c0 + w], fp_tgt[fc][:, 0:w]
                        last_ms = S.op("scalar", lambda e, d=cd, s=cs2:
                                       e.activation(d, s, AF.Copy), sem="s_act")
                    fc_ms[fc] = last_ms
            return last_ms, fc_ms

        S.dma("sync", masks.ap().rearrange("p (m f) -> p m f", f=128),
              masks_in.ap().rearrange("m p f -> p m f"), "d_sm")
        S.dma("sync", col_ampb.ap(), ampb_col.ap(), "d_sm")
        S.dma("sync", col_phb.ap(), phb_col.ap(), "d_sm")
        S.dma("sync", col_entpb.ap().rearrange("p (l f) -> p l f", f=16),
              entpb_col.ap().rearrange("l p f -> p l f"), "d_sm")
        S.dma("sync", col_pb.ap().rearrange("p (l q f) -> p l q f", q=2, f=16),
              pathb_col.ap().rearrange("l q p f -> p l q f"), "d_sm")
        S.dma("sync", colp.ap().rearrange("p (l q f) -> p l q f", q=2, f=16),
              phase_col.ap().rearrange("l q p f -> p l q f"), "d_sm")
        S.dma("sync", col_g3.ap().rearrange("p (l f) -> p l f", f=16),
              ln3_gcol.ap().rearrange("l p f -> p l f"), "d_sm")
        S.dma("sync", col_g4.ap().rearrange("p (l f) -> p l f", f=16),
              ln4_gcol.ap().rearrange("l p f -> p l f"), "d_sm")
        S.dma("sync", col_b3.ap().rearrange("p (l f) -> p l f", f=16),
              ln3_bcol.ap().rearrange("l p f -> p l f"), "d_sm")
        m_small = S.dma("sync", col_b4.ap().rearrange("p (l f) -> p l f", f=16),
                        ln4_bcol.ap().rearrange("l p f -> p l f"), "d_sm")
        m_entb = S.dma("gpsimd", col_entb.ap().rearrange("p (l f) -> p l f", f=8),
                       entb_col.ap().rearrange("l p f -> p l f"), "d_sme")

        m_wc = []
        for i, (wsrc, c0, cw) in enumerate([(w_qkvT, 0, 3 * D), (w_oT, 3 * D, D),
                                            (w_amp, 4 * D, D), (w_ph, 5 * D, D)]):
            m_wc.append(S.dma("gpsimd", wbf.ap()[:, c0:c0 + cw], wsrc.ap(),
                              f"d_wc{i}"))
        S.op("vector", lambda e: e.memset(ones_col_bf.ap(), 1.0))
        S.op("vector", lambda e: e.memset(pihalf_col.ap(), PI / 2))
        S.op("vector", lambda e: e.memset(eps1.ap(), 1e-5))
        S.op("vector", lambda e: e.memset(onesf_row, 1.0))
        S.op("vector", lambda e: e.memset(onesb_row, 1.0))
        m_sets = S.op("vector", lambda e: e.memset(rows2_bf, 1.0), sem="s_dve")

        S.wait("vector", m_small)
        m_ampbh = S.op("vector", lambda e: e.tensor_scalar_mul(
            col_ampb.ap(), col_ampb.ap(), 0.5), sem="s_dve")

        S.wait("gpsimd", m_small)
        S.op("gpsimd", lambda e: e.tensor_scalar(colw.ap(), colp.ap(), PI, None, ALU.is_gt))
        S.op("gpsimd", lambda e: e.tensor_scalar_mul(colw.ap(), colw.ap(), -2 * PI))
        S.op("gpsimd", lambda e: e.tensor_tensor(colw.ap(), colw.ap(), colp.ap(), ALU.add))
        S.op("gpsimd", lambda e: e.tensor_scalar(colm.ap(), colw.ap(), -PI, None, ALU.is_lt))
        S.op("gpsimd", lambda e: e.tensor_scalar_mul(colm.ap(), colm.ap(), 2 * PI))
        m_wrap = S.op("gpsimd", lambda e: e.tensor_tensor(
            colw.ap(), colm.ap(), colw.ap(), ALU.add), sem="s_pool")
        S.wait("scalar", m_wrap)
        S.wait("scalar", m_sets)
        m_sin = S.op("scalar", lambda e: e.activation(colm.ap(), colw.ap(), AF.Sin),
                     sem="s_act")
        S.wait("gpsimd", m_sin)
        S.op("gpsimd", lambda e: e.tensor_scalar(colp.ap(), colw.ap(), PI / 2, None,
                                                 ALU.is_gt))
        S.op("gpsimd", lambda e: e.tensor_scalar_mul(colp.ap(), colp.ap(), -2 * PI))
        m_wr2 = S.op("gpsimd", lambda e: e.tensor_tensor(
            colw.ap(), colp.ap(), colw.ap(), ALU.add), sem="s_pool")
        S.wait("scalar", m_wr2)
        m_cos = S.op("scalar", lambda e: e.activation(colp.ap(), colw.ap(), AF.Sin,
                                                      bias=pihalf_col.ap()), sem="s_act")
        S.wait("vector", m_cos)
        m_mod = S.op("vector", lambda e: e.tensor_tensor(colm.ap(), colm.ap(), colp.ap(),
                                                         ALU.add), sem="s_dve")
        S.wait("gpsimd", m_mod)
        for l in range(2):
            S.op("gpsimd", lambda e, l=l: e.tensor_tensor(
                colw.ap()[:, 0:16], col_pb.ap()[:, (2 * l) * 16:(2 * l + 1) * 16],
                colm.ap()[:, (2 * l) * 16:(2 * l + 1) * 16], ALU.mult))
            S.op("gpsimd", lambda e, l=l: e.tensor_tensor(
                colw.ap()[:, 16:32], col_pb.ap()[:, (2 * l + 1) * 16:(2 * l + 2) * 16],
                colm.ap()[:, (2 * l + 1) * 16:(2 * l + 2) * 16], ALU.mult))
            S.op("gpsimd", lambda e, l=l: e.tensor_tensor(
                col_ptld.ap()[:, l * 16:(l + 1) * 16], colw.ap()[:, 0:16],
                colw.ap()[:, 16:32], ALU.add))
        m_mr_st = []
        for l in range(2):
            for p in range(2):
                S.wait("gpsimd", m_mod)
                m_mr_st.append(S.dma(
                    "gpsimd",
                    modrow_d.ap()[2 * l + p, :].rearrange("(a p) -> p a", p=128),
                    colm.ap()[:, (2 * l + p) * 16:(2 * l + p) * 16 + 16],
                    "d_mw", allow_slow_non_contiguous=True))

        pw_cons = [None, None]
        stg_free = [None] * 4
        bstg_free = None
        fold_mm_last = None
        fold_store_last = None
        for l in range(2):
            wt_ms = []
            S.wait("gpsimd", fold_mm_last)
            for j in range(8):
                wt_ms.append(S.dma("gpsimd", FW_T[j][:, :],
                                   entwT_sl.ap()[l, j * 128:(j + 1) * 128, :], "d_fw"))
            m_bev = None
            for fs in range(4):
                slot = fs % 2
                S.wait("gpsimd", pw_cons[slot])
                pwdst = poolH[:, 2 * PWB[slot]: 2 * PWB[slot] + 8 * 512] \
                    .rearrange("p (jc f) -> p jc f", f=512)
                pwsrc = entpw_sl.ap()[l, :, fs * 512:(fs + 1) * 512] \
                    .rearrange("(jc p) f -> p jc f", p=128)
                m_pw = S.dma("gpsimd", pwdst, pwsrc, f"d_pw{slot}")
                S.waits("tensor", wt_ms)
                S.wait("tensor", m_pw)
                S.wait("tensor", m_entb)

                def pwtile(j, slot=slot):
                    base = 2 * PWB[slot] + j * 512
                    return poolH[:, base:base + 512]

                S.wait("tensor", bank_free.get("pf"))
                m_b = None
                for j in range(8):
                    m_b = mm(PF.ap()[0:1, 0:512],
                             col_entb.ap()[:, l * 8 + j:l * 8 + j + 1], pwtile(j),
                             start=(j == 0), stop=(j == 7),
                             sem="s_pe" if j == 7 else None)
                S.wait("scalar", m_b)
                S.wait("scalar", bstg_free)
                m_bev = S.op("scalar", lambda e, fs=fs: e.activation(
                    BSTG[:, fs * 512:(fs + 1) * 512], PF.ap()[0:1, 0:512], AF.Copy),
                    sem="s_act")
                bank_free["pf"] = m_bev
                for dc in range(KC):
                    bk = PM[dc % 2]
                    S.wait("tensor", bank_free.get(bk.name))
                    m_mmv = None
                    for j in range(8):
                        m_mmv = mm(bk.ap()[:, 0:512],
                                   FW_T[j][:, dc * 128:(dc + 1) * 128], pwtile(j),
                                   start=(j == 0), stop=(j == 7),
                                   sem="s_pe" if j == 7 else None)
                    fold_mm_last = m_mmv
                    sg = dc % 4
                    S.wait("scalar", stg_free[sg])
                    S.wait("scalar", m_mmv)
                    m_ev = S.op("scalar", lambda e, sg=sg, dc=dc: e.activation(
                        FSTG[sg][:, 0:512], PM[dc % 2].ap()[:, 0:512], AF.Copy),
                        sem="s_act")
                    bank_free[bk.name] = m_ev
                    S.wait("sync", m_ev)
                    m_stx = S.dma("sync",
                                  fold_in.ap()[l * FOLD_LROWS + dc * 128:
                                               l * FOLD_LROWS + (dc + 1) * 128,
                                               fs * 512:(fs + 1) * 512],
                                  FSTG[sg][:, 0:512], f"d_fst{sg}")
                    stg_free[sg] = m_stx
                    fold_store_last = m_stx
                pw_cons[slot] = fold_mm_last
            S.wait("sync", m_bev)
            S.wait("sync", bstg_free)
            fold_store_last = S.dma(
                "sync",
                fold_in.ap()[l * FOLD_LROWS + 2048: l * FOLD_LROWS + 2049, :],
                BSTG, "d_bst")
            bstg_free = fold_store_last

        S.waits("gpsimd", [S.mark("d_fst0"), S.mark("d_fst1"), S.mark("d_fst2"),
                           S.mark("d_fst3"), S.mark("d_bst")])
        m_ar = S.op("gpsimd", lambda e: e.collective_compute(
            "AllReduce", ALU.add, replica_groups=[list(range(NCORES))],
            ins=[fold_in.ap().opt()], outs=[fold_out.ap().opt()]), sem="s_cc")

        pf_state = {}

        def emit_pfold():
            pfold_dve_last = None
            pstg_store = [None, None]
            for l in range(2):
                mr_ld = []
                for p in range(2):
                    S.wait("sync", fold_mm_last)
                    S.wait("sync", S.mark("s_pe"))
                    S.waits("sync", m_mr_st)
                    mr_ld.append(S.dma("sync", MODROW[p], modrow_d.ap()[2 * l + p:2 * l + p + 1, :],
                                       "d_pf"))
                for rt in range(2):
                    S.wait("sync", fold_mm_last)
                    S.wait("sync", pfold_dve_last)
                    m_pw0 = S.dma("sync", PW0, pathw_sl.ap()[l, 0, rt * 128:(rt + 1) * 128, :],
                                  "d_pf")
                    m_pw1 = S.dma("sync", PW1, pathw_sl.ap()[l, 1, rt * 128:(rt + 1) * 128, :],
                                  "d_pf")
                    m_pf_all = S.mark("d_pf")
                    for fs in range(4):
                        S.wait("tensor", bank_free.get("pf"))
                        S.wait("tensor", m_pf_all)
                        m_k1 = mm(PF.ap()[:, 0:512], onesf_row,
                                  MODROW[0][:, fs * 512:(fs + 1) * 512],
                                  start=True, stop=True, sem="s_pe")
                        S.wait("vector", m_k1)
                        S.wait("vector", m_pf_all)
                        m_t1 = S.op("vector", lambda e, fs=fs: e.tensor_tensor(
                            TS[4][:, 0:512], PW0[:, fs * 512:(fs + 1) * 512],
                            PF.ap()[:, 0:512], ALU.mult), sem="s_dve")
                        bank_free["pf"] = m_t1
                        S.wait("tensor", m_t1)
                        m_k2 = mm(PF.ap()[:, 0:512], onesf_row,
                                  MODROW[1][:, fs * 512:(fs + 1) * 512],
                                  start=True, stop=True, sem="s_pe")
                        S.wait("vector", m_k2)
                        m_t2 = S.op("vector", lambda e, fs=fs: e.tensor_tensor(
                            TS[5][:, 0:512], PW1[:, fs * 512:(fs + 1) * 512],
                            PF.ap()[:, 0:512], ALU.mult), sem="s_dve")
                        bank_free["pf"] = m_t2
                        S.waits("vector", stg_free + [bstg_free, pstg_store[rt]])
                        pfold_dve_last = S.op("vector", lambda e, rt=rt, fs=fs: e.tensor_tensor(
                            PSTG[rt][:, fs * 512:(fs + 1) * 512], TS[4][:, 0:512],
                            TS[5][:, 0:512], ALU.add), sem="s_dve")
                    S.wait("sync", pfold_dve_last)
                    pstg_store[rt] = S.dma(
                        "sync", pg_in.ap()[l * 256 + rt * 128: l * 256 + (rt + 1) * 128, :],
                        PSTG[rt], f"d_fst{rt}")
            S.wait("gpsimd", pstg_store[0])
            S.wait("gpsimd", pstg_store[1])
            m_ag = S.op("gpsimd", lambda e: e.collective_compute(
                "AllGather", ALU.bypass, replica_groups=[list(range(NCORES))],
                ins=[pg_in.ap().opt()], outs=[pg_out.ap().opt()]), sem="s_cc")

            pf_state["dve"] = pfold_dve_last
            pf_state["st"] = [pstg_store[0], pstg_store[1]]

        tap_dram("mfold", fold_out.ap(), [S.mark("s_cc")])
        tap_dram("pg", pg_out.ap(), [S.mark("s_cc")])
        tap_dram("modrow", modrow_d.ap(), [S.mark("s_cc")])
        tap_dram("pgin", pg_in.ap(), [S.mark("s_cc")])
        tap_sbuf("colm", colm.ap(), [S.mark("s_cc")])

        tap_sbuf("ptld", col_ptld.ap(), [S.mark("s_dve")])
        m_Ufree = (fold_mm_last,)

        m_r1_store = [None] * NCH
        m_x_ld = None
        prev_attn_pe = None
        prev_qkv_pe = None
        attn_guard = {"relu3": None, "att3": None, "out2": None, "out3": None}
        m_aend = None

        for c in range(NCH):
            if c == 0:
                toff, wch, gc0 = 0, 640, 0
                ln_tt = [(0, 512), (512, 128)]
                kv_tt = [(0, 512), (512, 128)]
                v_tps = [0, 1, 2, 3, 4]
            else:
                toff, wch, gc0 = 128, 512, 128 + c * 512
                ln_tt = [(128, 512)]
                kv_tt = [(128, 512)]
                v_tps = [1, 2, 3, 4]

            S.wait("sync", m_r1_store[c - 1] if c > 0 else None)
            xdst = poolF[:, O_XR:O_XR + 10240] \
                .rearrange("p (fc t) -> p fc t", t=640)[:, :, toff:toff + wch]
            xsrc = x_fm.ap()[:, gc0:gc0 + wch].rearrange("(fc p) t -> p fc t", p=128)
            m_x_a = S.dma("sync", xdst[:, 0:8], xsrc[:, 0:8], "d_x")
            m_x_ld = S.dma("sync", xdst[:, 8:16], xsrc[:, 8:16], "d_x")

            m_tail = None
            if c > 0:
                S.wait("gpsimd", prev_attn_pe)
                for fc in range(KC):
                    S.op("gpsimd", lambda e, fc=fc: e.tensor_copy(
                        KBF[fc][:, 0:128], KBF[fc][:, 512:640]))
                m_tail = S.op("gpsimd", lambda e: e.tensor_copy(
                    VBF[0][:, :], VBF[4][:, :]), sem="s_pool")

            ln_pre = [prev_qkv_pe]
            x_fcms = [m_x_a] * 8 + [m_x_ld] * 8
            m_lnx, m_lnx_fc = emit_ln(XR, LNBF, None, ln_tt, None, None,
                                      pre_ms=ln_pre, pre_fc=x_fcms)
            if c == 0:
                tap_sbuf("lnx0", poolH[:, 2 * O_LNBF: 2 * O_LNBF + 10240], [m_lnx])

            m_qkv_ev = []
            for fg in range(12):
                slot = fg % 2
                S.wait("sync" if slot == 0 else "scalar", m_wc[0])
                m_slab = slab_load(slot, wbf.ap()[:, fg * 512:(fg + 1) * 512])
                S.wait("tensor", m_slab)
                S.wait("tensor", m_lnx)
                if fg < 8:
                    is_q = fg < 4
                    for fcg in range(4):
                        fc = (fg % 4) * 4 + fcg
                        bk = PM[fcg]
                        tts = [(128, 512)] if is_q else kv_tt
                        for (t0, tw) in tts:
                            S.wait("tensor", bank_free.get(bk.name))
                            m_g = None
                            for kc in range(KC):
                                S.wait("tensor", m_lnx_fc[kc])
                                m_g = mm(bk.ap()[:, 0:tw],
                                         slab_tile(slot, kc, fcg * 128, 128),
                                         LNBF[kc][:, t0:t0 + tw],
                                         start=(kc == 0), stop=(kc == KC - 1),
                                         sem="s_pe" if kc == KC - 1 else None)
                            if is_q:
                                dst = QBF[fc][:, t0 - 128:t0 - 128 + tw]
                            else:
                                dst = KBF[fc][:, t0:t0 + tw]
                                S.waits("scalar", m_Ufree)
                                if m_tail:
                                    S.wait("scalar", m_tail)
                            S.wait("scalar", m_g)
                            m_e = S.op("scalar", lambda e, d=dst, b=bk, tw=tw:
                                       e.activation(d, b.ap()[:, 0:tw], AF.Copy),
                                       sem="s_act")
                            bank_free[bk.name] = m_e
                            m_qkv_ev.append(m_e)
                else:
                    vs = fg - 8
                    for tp in v_tps:
                        bk = PM[tp % 2]
                        S.wait("tensor", bank_free.get(bk.name))
                        m_g = None
                        for kc in range(KC):
                            S.wait("tensor", m_lnx_fc[kc])
                            m_g = mm(bk.ap()[:, 0:512],
                                     LNBF[kc][:, tp * 128:(tp + 1) * 128],
                                     slab_tile(slot, kc, 0, 512),
                                     start=(kc == 0), stop=(kc == KC - 1),
                                     sem="s_pe" if kc == KC - 1 else None)
                        S.waits("scalar", m_Ufree)
                        if m_tail:
                            S.wait("scalar", m_tail)
                        S.wait("scalar", m_g)
                        m_e = S.op("scalar", lambda e, tp=tp, vs=vs, b=bk: e.activation(
                            VBF[tp][:, vs * 512:(vs + 1) * 512], b.ap()[:, 0:512],
                            AF.Copy), sem="s_act")
                        bank_free[bk.name] = m_e
                        m_qkv_ev.append(m_e)
                slab_cons[slot] = S.mark("s_pe")
            prev_qkv_pe = S.mark("s_pe")
            if c == 0:
                tap_sbuf("qbf0", poolH[:, 2 * O_QBF: 2 * O_QBF + 8192], m_qkv_ev[-1:])
                tap_sbuf("kbf0", poolH[:, 2 * U: 2 * U + 10240], m_qkv_ev[-1:])
                tap_sbuf("vbf0", poolH[:, 2 * (U + 5120): 2 * (U + 5120) + 10240],
                         m_qkv_ev[-1:])

            m_sim = [None] * 4
            m_maskread = [None] * 4
            m_att = [None] * 4
            m_outpe = [None] * 4
            m_relu = [None] * 4

            def emit_sims(j):
                bk = PA[j % 2]
                S.wait("tensor", bank_free.get(bk.name))
                S.waits("tensor", m_qkv_ev)
                last = None
                for dc in range(KC):
                    mm(bk.ap()[:, 0:128], KBF[dc][:, j * 128:j * 128 + 128],
                       QBF[dc][:, j * 128:(j + 1) * 128],
                       start=(dc == 0), stop=(dc == KC - 1))
                for dc in range(KC):
                    last = mm(bk.ap()[:, 128:256],
                              KBF[dc][:, 128 + j * 128:128 + j * 128 + 128],
                              QBF[dc][:, j * 128:(j + 1) * 128],
                              start=(dc == 0), stop=(dc == KC - 1),
                              sem="s_pe" if dc == KC - 1 else None)
                m_sim[j] = last

            def emit_attn_ew(j):
                bk = PA[j % 2]
                mprev = mview(0) if (c == 0 and j == 0) else mview(1)
                S.wait("vector", m_small)
                S.wait("vector", m_sim[j])
                if j >= 1:
                    S.wait("vector", m_relu[j - 1])
                else:
                    S.wait("vector", attn_guard["relu3"])
                S.op("vector", lambda e, b=bk, m=mprev: e.tensor_tensor(
                    S1[:, 0:128], b.ap()[:, 0:128], m, ALU.mult))
                m_mr = S.op("vector", lambda e, b=bk: e.tensor_tensor(
                    S1[:, 128:256], b.ap()[:, 128:256], mview(2), ALU.mult),
                    sem="s_dve")
                m_maskread[j] = m_mr
                bank_free[bk.name] = m_mr
                S.wait("scalar", m_mr)
                if j >= 1:
                    S.wait("scalar", m_att[j - 1])
                else:
                    S.wait("scalar", attn_guard["att3"])
                m_rl = S.op("scalar", lambda e: e.activation(
                    S2[:, 0:256], S1[:, 0:256], AF.Relu, scale=1.0 / 128.0),
                    sem="s_act")
                m_relu[j] = m_rl
                S.wait("vector", m_rl)
                if j >= 2:
                    S.wait("vector", m_outpe[j - 2])
                else:
                    S.wait("vector", attn_guard["out2"] if j == 0 else attn_guard["out3"])
                m_sq = S.op("vector", lambda e, j=j: e.tensor_tensor(
                    ATT[j % 2][:, 0:256], S2[:, 0:256], S2[:, 0:256], ALU.mult),
                    sem="s_dve")
                m_att[j] = m_sq

            def emit_outs(j):
                S.wait("tensor", m_att[j])
                last = None
                for dc in range(KC):
                    bk = PM[dc % 2]
                    S.wait("tensor", bank_free.get(bk.name))
                    mm(bk.ap()[:, 0:128], VBF[j][:, dc * 128:(dc + 1) * 128],
                       ATT[j % 2][:, 0:128], start=True, stop=False)
                    last = mm(bk.ap()[:, 0:128], VBF[j + 1][:, dc * 128:(dc + 1) * 128],
                              ATT[j % 2][:, 128:256], start=False, stop=True,
                              sem="s_pe")
                    S.wait("scalar", last)
                    m_e = S.op("scalar", lambda e, dc=dc, j=j, b=bk: e.activation(
                        ARAW[dc][:, j * 128:(j + 1) * 128], b.ap()[:, 0:128], AF.Copy),
                        sem="s_act")
                    bank_free[bk.name] = m_e
                m_outpe[j] = last

            emit_sims(0)
            emit_attn_ew(0)
            emit_sims(1)
            emit_attn_ew(1)
            emit_outs(0)
            emit_sims(2)
            emit_attn_ew(2)
            emit_outs(1)
            emit_sims(3)
            emit_attn_ew(3)
            emit_outs(2)
            emit_outs(3)
            attn_guard["relu3"] = m_relu[3]
            attn_guard["att3"] = m_att[3]
            attn_guard["out2"] = m_outpe[2]
            attn_guard["out3"] = m_outpe[3]
            m_att_act = S.mark("s_act")
            if c == 0:
                tap_sbuf("araw0", poolH[:, 2 * (U + 10240): 2 * (U + 10240) + 8192],
                         [m_att_act])

            m_r1 = []
            for fg in range(4):
                slot = fg % 2
                S.wait("sync" if slot == 0 else "scalar", m_wc[1])
                m_slab = slab_load(slot, wbf.ap()[:, 3 * D + fg * 512:
                                                  3 * D + (fg + 1) * 512])
                S.wait("tensor", m_slab)
                S.wait("tensor", m_att_act)
                for fcg in range(4):
                    fc = fg * 4 + fcg
                    bk = PM[fcg]
                    S.wait("tensor", bank_free.get(bk.name))
                    m_g = None
                    for kc in range(KC):
                        m_g = mm(bk.ap()[:, 0:512],
                                 slab_tile(slot, kc, fcg * 128, 128),
                                 ARAW[kc][:, 0:512],
                                 start=(kc == 0), stop=(kc == KC - 1),
                                 sem="s_pe" if kc == KC - 1 else None)
                    S.wait("vector", m_g)
                    S.wait("vector", m_x_ld)
                    m_add = S.op("vector", lambda e, fc=fc, b=bk: e.tensor_tensor(
                        XR[fc][:, 128:640], b.ap()[:, 0:512], XR[fc][:, 128:640],
                        ALU.add), sem="s_dve")
                    bank_free[bk.name] = m_add
                    m_r1.append(m_add)
                slab_cons[slot] = S.mark("s_pe")
            r1dst = r1s.ap()[:, c * 512:(c + 1) * 512] \
                .rearrange("(fc p) t -> p fc t", p=128)
            r1src = poolF[:, O_XR:O_XR + 10240] \
                .rearrange("p (fc t) -> p fc t", t=640)[:, :, 128:640]
            S.wait("sync", m_r1[7])
            S.dma("sync", r1dst[:, 0:8], r1src[:, 0:8], "d_st")
            S.wait("sync", m_r1[15])
            m_r1_store[c] = S.dma("sync", r1dst[:, 8:16], r1src[:, 8:16], "d_st")
            prev_attn_pe = S.mark("s_pe")

        m_aend = S.mark("s_pe")
        emit_pfold()
        m_b_gate = [m_aend, pf_state["dve"]] + pf_state["st"]
        tap_dram("r1", r1s.ap(), [m_r1_store[NCH - 1]])

        S.wait("gpsimd", ("s_cc", 1))
        m_bt = []
        for l in range(2):
            m_bt.append(S.dma(
                "gpsimd", col_btot.ap()[:, l * 16:(l + 1) * 16],
                fold_out.ap()[l * FOLD_LROWS + 2048, :].rearrange("(a p) -> p a", p=128),
                "d_bt", allow_slow_non_contiguous=True))
        S.waits("gpsimd", m_bt)
        S.wait("gpsimd", m_small)
        m_btot = S.op("gpsimd", lambda e: e.tensor_tensor(
            col_btot.ap(), col_entpb.ap(), col_btot.ap(), ALU.add), sem="s_pool")
        tap_sbuf("btot", col_btot.ap(), [m_btot])

        m_out_store = None
        for c in range(NCH):
            S.wait("sync", m_out_store)
            S.wait("sync", m_r1_store[NCH - 1])
            r1src = r1s.ap()[:, c * 512:(c + 1) * 512] \
                .rearrange("(fc p) t -> p fc t", p=128)
            xdst = poolF[:, O_XR:O_XR + 10240] \
                .rearrange("p (fc t) -> p fc t", t=640)[:, :, 128:640]
            m_r1ld_a = S.dma("sync", xdst[:, 0:8], r1src[:, 0:8], "d_x")
            m_r1ld = S.dma("sync", xdst[:, 8:16], r1src[:, 8:16], "d_x")

            r1_fcms = [m_r1ld_a] * 8 + [m_r1ld] * 8
            m_ln2, m_ln2_fc = emit_ln(XR, LNBF, None, [(128, 512)], None, None,
                                      pre_ms=[m_aend], pre_fc=r1_fcms)
            if c == 0:
                tap_sbuf("lnr10", poolH[:, 2 * O_LNBF: 2 * O_LNBF + 10240], [m_ln2])

            m_amp_ev = [None] * KC
            m_ph_ev = [None] * KC
            for fg in range(8):
                slot = fg % 2
                wbase = 4 * D if fg < 4 else 5 * D
                S.wait("sync" if slot == 0 else "scalar", m_wc[2 if fg < 4 else 3])
                m_slab = slab_load(slot, wbf.ap()[:, wbase + (fg % 4) * 512:
                                                  wbase + (fg % 4 + 1) * 512])
                S.wait("tensor", m_slab)
                for fcg in range(4):
                    fc = (fg % 4) * 4 + fcg
                    bk = PM[fcg]
                    S.wait("tensor", bank_free.get(bk.name))
                    m_g = None
                    for kc in range(KC):
                        S.wait("tensor", m_ln2_fc[kc])
                        m_g = mm(bk.ap()[:, 0:512],
                                 slab_tile(slot, kc, fcg * 128, 128),
                                 LNBF[kc][:, 128:640],
                                 start=(kc == 0), stop=(kc == KC - 1),
                                 sem="s_pe" if kc == KC - 1 else None)
                    S.wait("scalar", m_g)
                    S.waits("scalar", m_b_gate + [m_ampbh, m_small])
                    if fg < 4:
                        m_e = S.op("scalar", lambda e, fc=fc, b=bk: e.activation(
                            HA[fc][:, :], b.ap()[:, 0:512], AF.Tanh, scale=0.5,
                            bias=col_ampb.ap()[:, fc:fc + 1]), sem="s_act")
                        m_amp_ev[fc] = m_e
                    else:
                        m_e = S.op("scalar", lambda e, fc=fc, b=bk: e.activation(
                            HB[fc][:, :], b.ap()[:, 0:512], AF.Tanh,
                            bias=col_phb.ap()[:, fc:fc + 1]), sem="s_act")
                        m_ph_ev[fc] = m_e
                    bank_free[bk.name] = m_e
                slab_cons[slot] = S.mark("s_pe")

            m_chain_prev = None
            m_h0cast = []
            for fc in range(KC):
                S.wait("gpsimd", m_ph_ev[fc])
                S.wait("gpsimd", m_chain_prev)
                S.op("gpsimd", lambda e, fc=fc: e.tensor_scalar(
                    TS[2][:, :], HB[fc][:, :], 0.5, None, ALU.is_gt))
                S.op("gpsimd", lambda e: e.tensor_scalar_mul(TS[2][:, :], TS[2][:, :], -2.0))
                m_z = S.op("gpsimd", lambda e, fc=fc: e.tensor_tensor(
                    TS[2][:, :], TS[2][:, :], HB[fc][:, :], ALU.add), sem="s_pool")
                S.wait("scalar", m_ph_ev[fc])
                if m_chain_prev:
                    S.wait("scalar", m_chain_prev)
                m_s = S.op("scalar", lambda e, fc=fc: e.activation(
                    TS[3][:, :], HB[fc][:, :], AF.Sin, scale=PI), sem="s_act")
                S.wait("scalar", m_z)
                S.wait("scalar", m_sets)
                m_cs0 = S.op("scalar", lambda e: e.activation(
                    TS[4][:, :], TS[2][:, :], AF.Sin, scale=PI,
                    bias=pihalf_col.ap()), sem="s_act")
                S.wait("vector", m_cs0)
                m_cs = S.op("vector", lambda e: e.tensor_tensor(
                    TS[5][:, :], TS[3][:, :], TS[4][:, :], ALU.add), sem="s_dve")
                S.wait("gpsimd", m_amp_ev[fc])
                m_av = S.op("gpsimd", lambda e, fc=fc: e.tensor_scalar(
                    HA[fc][:, :], HA[fc][:, :], 1.0, 0.5, ALU.add, ALU.mult),
                    sem="s_pool")
                S.wait("vector", m_cs)
                S.wait("vector", m_av)
                m_h0 = S.op("vector", lambda e, fc=fc: e.tensor_tensor(
                    HA[fc][:, :], TS[5][:, :], HA[fc][:, :], ALU.mult), sem="s_dve")
                m_chain_prev = m_h0
                S.wait("gpsimd", m_h0)
                m_cc = S.op("gpsimd", lambda e, fc=fc: e.tensor_copy(
                    QBF[fc][:, :], HA[fc][:, :]), sem="s_pool")
                m_h0cast.append(m_cc)
            if c == 0:
                tms = tap_sbuf("h0", poolF[:, U:U + 8192], [m_chain_prev])
                if tms:
                    m_h0cast = m_h0cast + [tms]

            hcur, hnxt = HA, HB
            m_hbf = list(m_h0cast[:KC])
            m_hcur = [m_chain_prev]
            for l in range(2):
                m_pre = []
                for fg in range(4):
                    slot = fg % 2
                    S.wait("sync" if slot == 0 else "scalar", ("s_cc", 1))
                    m_slab = slab_load(
                        slot,
                        fold_out.ap()[l * FOLD_LROWS: l * FOLD_LROWS + 2048,
                                      fg * 512:(fg + 1) * 512])
                    S.wait("tensor", m_slab)
                    for fcg in range(4):
                        fc = fg * 4 + fcg
                        bk = PM[fcg]
                        S.wait("tensor", bank_free.get(bk.name))
                        m_g = None
                        for kc in range(KC):
                            S.wait("tensor", m_hbf[kc])
                            m_g = mm(bk.ap()[:, 0:512],
                                     slab_tile(slot, kc, fcg * 128, 128),
                                     QBF[kc][:, :],
                                     start=(kc == 0), stop=(kc == KC - 1),
                                     sem="s_pe" if kc == KC - 1 else None)
                        S.wait("vector", m_g)
                        S.wait("vector", m_btot)
                        S.waits("vector", m_hcur)
                        m_p = S.op("vector", lambda e, fc=fc, l=l, b=bk, hc=hcur, hn=hnxt:
                                   e.scalar_tensor_tensor(
                                       hn[fc][:, :], b.ap()[:, 0:512],
                                       col_btot.ap()[:, l * 16 + fc:l * 16 + fc + 1],
                                       hc[fc][:, :], ALU.add, ALU.add), sem="s_dve")
                        bank_free[bk.name] = m_p
                        m_pre.append(m_p)
                    slab_cons[slot] = S.mark("s_pe")
                if c == 0 and l == 0:
                    tms = tap_sbuf("pre3", poolF[:, U + 8192:U + 16384], m_pre[-1:])
                    if tms:
                        m_pre = m_pre + [tms]
                hcur, hnxt = hnxt, hcur
                m_ln3, m_ln3_fc = emit_ln(hcur, QBF, hnxt, [(0, 512)],
                                          col_g3.ap()[:, l * 16:(l + 1) * 16],
                                          col_b3.ap()[:, l * 16:(l + 1) * 16],
                                          pre_ms=m_pre)
                hcur, hnxt = hnxt, hcur
                m_hbf = list(m_ln3_fc)
                m_hcur = [m_ln3]
                if c == 0 and l == 0:
                    tms = tap_sbuf("h1", poolF[:, U:U + 8192], [m_ln3])
                    if tms:
                        m_hbf = m_hbf + [tms]
                        m_hcur = m_hcur + [tms]
                if c == 0 and l == 1:
                    tms = tap_sbuf("pre3b", poolF[:, U + 8192:U + 16384], m_pre[-1:])
                    if tms:
                        m_pre = m_pre + [tms]
                if c == 0 and l == 0:
                    tq = tap_sbuf("qbfa", poolH[:, 2 * O_QBF: 2 * O_QBF + 8192], m_hbf)
                    if tq:
                        m_hbf = m_hbf + [tq]
                m_pre4 = []
                for fg in range(4):
                    slot = fg % 2
                    pq = "sync" if slot == 0 else "scalar"
                    S.wait(pq, ("s_cc", 2))
                    S.wait(pq, slab_cons[slot])
                    m_slab = None
                    for h in range(2):
                        for rr in range(8):
                            row0 = rr * 512 + l * 256 + h * 128
                            src = bass.AP(pg_out, row0 * D + fg * 512,
                                          [[D, 128], [1, 512]])
                            m_slab = S.dma(pq, slab_tile(slot, h * 8 + rr, 0, 512),
                                           src, f"d_sl{slot}s")
                    S.wait("tensor", m_slab)
                    for fcg in range(4):
                        fc = fg * 4 + fcg
                        bk = PM[fcg]
                        S.wait("tensor", bank_free.get(bk.name))
                        m_g = None
                        for kc in range(KC):
                            pos = (kc % 2) * 8 + kc // 2
                            S.wait("tensor", m_hbf[kc])
                            m_g = mm(bk.ap()[:, 0:512],
                                     slab_tile(slot, pos, fcg * 128, 128),
                                     QBF[kc][:, :],
                                     start=(kc == 0), stop=(kc == KC - 1),
                                     sem="s_pe" if kc == KC - 1 else None)
                        S.wait("vector", m_g)
                        S.waits("vector", m_hcur)
                        m_p = S.op("vector", lambda e, fc=fc, l=l, b=bk, hc=hcur, hn=hnxt:
                                   e.scalar_tensor_tensor(
                                       hn[fc][:, :], b.ap()[:, 0:512],
                                       col_ptld.ap()[:, l * 16 + fc:l * 16 + fc + 1],
                                       hc[fc][:, :], ALU.add, ALU.add), sem="s_dve")
                        bank_free[bk.name] = m_p
                        m_pre4.append(m_p)
                    slab_cons[slot] = S.mark("s_pe")
                if c == 0 and l == 0:
                    tq2 = tap_sbuf("qbfb", poolH[:, 2 * O_QBF: 2 * O_QBF + 8192],
                                   m_pre4[-1:])
                    if tq2:
                        m_pre4 = m_pre4 + [tq2]
                hcur, hnxt = hnxt, hcur
                m_ln4, m_ln4_fc = emit_ln(hcur, QBF, hnxt, [(0, 512)],
                                          col_g4.ap()[:, l * 16:(l + 1) * 16],
                                          col_b4.ap()[:, l * 16:(l + 1) * 16],
                                          pre_ms=m_pre4)
                hcur, hnxt = hnxt, hcur
                m_hbf = list(m_ln4_fc)
                m_hcur = [m_ln4]
                if c == 0 and l == 0:
                    t1 = tap_sbuf("pre4", poolF[:, U + 8192:U + 16384], m_pre4[-1:])
                    t2 = tap_sbuf("h2", poolF[:, U:U + 8192], [m_ln4])
                    for t in (t1, t2):
                        if t:
                            m_hbf = m_hbf + [t]
                            m_hcur = m_hcur + [t]
                if c == 0 and l == 1:
                    t3 = tap_sbuf("h4", poolF[:, U:U + 8192], [m_ln4])
                    if t3:
                        m_hcur = m_hcur + [t3]

            m_fin = []
            for fc in range(KC):
                S.wait("vector", m_ln4_fc[fc])
                S.waits("vector", m_hcur[1:])
                S.wait("vector", m_r1ld)
                m_f = S.op("vector", lambda e, fc=fc, hc=hcur: e.tensor_tensor(
                    XR[fc][:, 128:640], hc[fc][:, :], XR[fc][:, 128:640], ALU.add),
                    sem="s_dve")
                m_fin.append(m_f)
            odst = out_fm.ap()[:, c * 512:(c + 1) * 512] \
                .rearrange("(fc p) t -> p fc t", p=128)
            osrc = poolF[:, O_XR:O_XR + 10240] \
                .rearrange("p (fc t) -> p fc t", t=640)[:, :, 128:640]
            S.wait("sync", m_fin[7])
            S.dma("sync", odst[:, 0:8], osrc[:, 0:8], "d_st")
            S.wait("sync", m_fin[15])
            m_out_store = S.dma("sync", odst[:, 8:16], osrc[:, 8:16], "d_st")

        S.wait("sync", S.mark("d_st"))
        S.wait("sync", S.mark("d_tap"))

        with nc.Block() as blk:
            @blk.sync
            def _(eng):
                for fn in S.q["sync"]:
                    fn(eng)

            @blk.gpsimd
            def _(eng):
                for fn in S.q["gpsimd"]:
                    fn(eng)

            @blk.vector
            def _(eng):
                for fn in S.q["vector"]:
                    fn(eng)

            @blk.scalar
            def _(eng):
                for fn in S.q["scalar"]:
                    fn(eng)

            @blk.tensor
            def _(eng):
                for fn in S.q["tensor"]:
                    fn(eng)

    return nc


_BUILD_CACHE = {}


def _host_inputs(inputs):
    x = np.asarray(inputs["L0_attn_x"], dtype=np.float32)
    w_attn = np.asarray(inputs["L0_attn_w_attn"], dtype=np.float32)
    w_o = np.asarray(inputs["L0_attn_w_o"], dtype=np.float32)
    amp_w = np.asarray(inputs["L0_ffn_amp_w"], dtype=np.float32)
    amp_b = np.asarray(inputs["L0_ffn_amp_b"], dtype=np.float32)
    ph_w = np.asarray(inputs["L0_ffn_ph_w"], dtype=np.float32)
    ph_b = np.asarray(inputs["L0_ffn_ph_b"], dtype=np.float32)
    ent_w = np.asarray(inputs["L0_ffn_ent_w"], dtype=np.float32)
    ent_b = np.asarray(inputs["L0_ffn_ent_b"], dtype=np.float32)
    ent_pw = np.asarray(inputs["L0_ffn_ent_pw"], dtype=np.float32)
    ent_pb = np.asarray(inputs["L0_ffn_ent_pb"], dtype=np.float32)
    path_w = np.asarray(inputs["L0_ffn_path_w"], dtype=np.float32)
    path_b = np.asarray(inputs["L0_ffn_path_b"], dtype=np.float32)
    phase = np.asarray(inputs["L0_ffn_phase"], dtype=np.float32)
    ln3_g = np.asarray(inputs["L0_ffn_ln3_g"], dtype=np.float32)
    ln3_b = np.asarray(inputs["L0_ffn_ln3_b"], dtype=np.float32)
    ln4_g = np.asarray(inputs["L0_ffn_ln4_g"], dtype=np.float32)
    ln4_b = np.asarray(inputs["L0_ffn_ln4_b"], dtype=np.float32)

    toks = np.ascontiguousarray(x.reshape(2 * 8192, D))

    def colize(v, n):
        return np.ascontiguousarray(v.reshape(n, 128).T)

    shared = {
        "w_qkvT": np.ascontiguousarray(w_attn.T),
        "w_oT": np.ascontiguousarray(w_o.T),
        "w_amp": np.ascontiguousarray(amp_w),
        "w_ph": np.ascontiguousarray(ph_w),
        "entpb_col": np.ascontiguousarray(
            np.stack([colize(ent_pb[l], 16) for l in range(2)])),
        "pathb_col": np.ascontiguousarray(
            np.stack([[colize(path_b[l, p], 16) for p in range(2)] for l in range(2)])),
        "phase_col": np.ascontiguousarray(
            np.stack([[colize(phase[l, p], 16) for p in range(2)] for l in range(2)])),
        "ampb_col": colize(amp_b, 16),
        "phb_col": colize(ph_b, 16),
        "ln3_gcol": np.ascontiguousarray(np.stack([colize(ln3_g[l], 16)
                                                   for l in range(2)])),
        "ln4_gcol": np.ascontiguousarray(np.stack([colize(ln4_g[l], 16)
                                                   for l in range(2)])),
        "ln3_bcol": np.ascontiguousarray(np.stack([colize(ln3_b[l], 16)
                                                   for l in range(2)])),
        "ln4_bcol": np.ascontiguousarray(np.stack([colize(ln4_b[l], 16)
                                                   for l in range(2)])),
    }
    sidx = np.arange(128)
    m_prev = (sidx[:, None] >= sidx[None, :]).astype(np.float32)
    m_cur = (sidx[:, None] <= sidx[None, :]).astype(np.float32)
    entwT = np.ascontiguousarray(np.transpose(ent_w, (0, 2, 1)))

    in_maps = []
    for c in range(NCORES):
        r0 = c * TPC
        xs = np.zeros((HALO + TPC, D), dtype=np.float32)
        if c in (0, 4):
            xs[HALO:] = toks[r0:r0 + TPC]
            m0 = np.zeros((128, 128), dtype=np.float32)
        else:
            xs[:] = toks[r0 - HALO:r0 + TPC]
            m0 = m_prev
        j0 = c * JSL
        d0 = c * 256
        im = dict(shared)
        im["x_fm"] = np.ascontiguousarray(xs.T)
        im["entwT_sl"] = np.ascontiguousarray(entwT[:, j0:j0 + JSL, :])
        im["entpw_sl"] = np.ascontiguousarray(ent_pw[:, j0:j0 + JSL, :])
        im["entb_col"] = np.ascontiguousarray(
            np.stack([colize(ent_b[l, j0:j0 + JSL], 8) for l in range(2)]))
        im["pathw_sl"] = np.ascontiguousarray(path_w[:, :, d0:d0 + 256, :])
        im["masks"] = np.ascontiguousarray(np.stack([m0, m_prev, m_cur]))
        in_maps.append(im)
    return in_maps


def kernel(**inputs):
    in_maps = _host_inputs(inputs)
    key = "main"
    if key not in _BUILD_CACHE:
        _BUILD_CACHE[key] = build()
    nc = _BUILD_CACHE[key]
    res = run_bass_kernel_spmd(nc, in_maps, core_ids=list(range(NCORES)))
    out = np.empty((2 * 8192, D), dtype=np.float32)
    for c in range(NCORES):
        out[c * TPC:(c + 1) * TPC] = np.asarray(res.results[c]["out_fm"]).T
    return out.reshape(2, 8192, D)


# version 12
# speedup vs baseline: 1.3989x; 1.3989x over previous
import numpy as np
import concourse.bass as bass
import concourse.mybir as mybir
from concourse.bass_utils import run_bass_kernel_spmd
from contextlib import ExitStack

F32 = mybir.dt.float32
BF16 = mybir.dt.bfloat16
AF = mybir.ActivationFunctionType
ALU = mybir.AluOpType
PI = float(np.pi)

NCORES = 8
D = 2048
KC = 16
TPC = 2048
HALO = 128
CH = 512
NCH = TPC // CH
JSL = 1024
FOLD_LROWS = 2049
INVN = 1.0 / 2048.0

ENGINES = ("sync", "gpsimd", "vector", "scalar", "tensor")


class Sched:
    def __init__(self, sems):
        self.q = {e: [] for e in ENGINES}
        self.sems = sems
        self.cnt = {name: 0 for name in sems}
        self.hw = {e: {name: 0 for name in sems} for e in ENGINES}

    ENG_SEM = {"vector": "s_dve", "scalar": "s_act", "gpsimd": "s_pool"}

    def op(self, eng, fn, sem=None, amt=1, is_dma=False):
        es = None if is_dma else self.ENG_SEM.get(eng)
        if es is not None and sem != "s_cc":
            self.wait(eng, (es, self.cnt[es]))
            self.cnt[es] += 1
            h = self.sems[es]
            self.q[eng].append(lambda e, fn=fn, h=h: fn(e).then_inc(h, 1))
            return (es, self.cnt[es])
        if sem is None:
            self.q[eng].append(fn)
            return None
        if es is not None and sem == "s_cc":
            self.wait(eng, (es, self.cnt[es]))
        self.cnt[sem] += amt
        h = self.sems[sem]
        self.q[eng].append(lambda e, fn=fn, h=h, a=amt: fn(e).then_inc(h, a))
        return (sem, self.cnt[sem])

    def mark(self, sem):
        return (sem, self.cnt[sem])

    def wait(self, eng, ms):
        if ms is None:
            return
        sem, val = ms
        if self.hw[eng][sem] >= val:
            return
        self.hw[eng][sem] = val
        h = self.sems[sem]
        self.q[eng].append(lambda e, h=h, v=val: e.wait_ge(h, v))

    def waits(self, eng, mss):
        best = {}
        for ms in mss:
            if ms is None:
                continue
            sem, val = ms
            if best.get(sem, -1) < val:
                best[sem] = val
        for sem, val in best.items():
            self.wait(eng, (sem, val))

    def dma(self, eng, out_ap, in_ap, sem, **kw):
        return self.op(eng, lambda e: e.dma_start(out_ap, in_ap, **kw), sem, 16,
                       is_dma=True)


def build(taps=()):
    taps = set(taps)
    nc = bass.Bass("TRN2", target_bir_lowering=False)

    def inp(name, shape, dtype=F32):
        return nc.dram_tensor(name, shape, dtype, kind="ExternalInput")

    x_fm = inp("x_fm", [D, HALO + TPC])
    w_qkvT = inp("w_qkvT", [D, 3 * D])
    w_oT = inp("w_oT", [D, D])
    w_amp = inp("w_amp", [D, D])
    w_ph = inp("w_ph", [D, D])
    entwT_sl = inp("entwT_sl", [2, JSL, D])
    entpw_sl = inp("entpw_sl", [2, JSL, D])
    entb_col = inp("entb_col", [2, 128, JSL // 128])
    entpb_col = inp("entpb_col", [2, 128, 16])
    pathw_sl = inp("pathw_sl", [2, 2, 256, D])
    pathb_col = inp("pathb_col", [2, 2, 128, 16])
    phase_col = inp("phase_col", [2, 2, 128, 16])
    ampb_col = inp("ampb_col", [128, 16])
    phb_col = inp("phb_col", [128, 16])
    ln3_gcol = inp("ln3_gcol", [2, 128, 16])
    ln4_gcol = inp("ln4_gcol", [2, 128, 16])
    ln3_bcol = inp("ln3_bcol", [2, 128, 16])
    ln4_bcol = inp("ln4_bcol", [2, 128, 16])
    masks_in = inp("masks", [3, 128, 128])

    out_fm = nc.dram_tensor("out_fm", [D, TPC], F32, kind="ExternalOutput")
    wbf = nc.dram_tensor("wbf", [D, 6 * D], BF16, kind="Internal")
    r1s = nc.dram_tensor("r1s", [D, TPC], F32, kind="Internal")
    fold_in = nc.dram_tensor("fold_in", [2 * FOLD_LROWS, D], BF16, kind="Internal")
    fold_out = nc.dram_tensor("fold_out", [2 * FOLD_LROWS, D], BF16, kind="Internal")
    modrow_d = nc.dram_tensor("modrow_d", [4, D], F32, kind="Internal")
    pg_in = nc.dram_tensor("pg_in", [512, D], BF16, kind="Internal")
    pg_out = nc.dram_tensor("pg_out", [NCORES * 512, D], BF16, kind="Internal")

    tap_t = {}
    tapdefs = {
        "lnx0": ([128, 10240], BF16), "r1": ([D, TPC], F32),
        "mfold": ([2 * FOLD_LROWS, D], BF16), "pg": ([NCORES * 512, D], BF16),
        "h0": ([128, 8192], F32), "pre3": ([128, 8192], F32),
        "h1": ([128, 8192], F32), "araw0": ([128, 8192], BF16),
        "qbf0": ([128, 8192], BF16), "kbf0": ([128, 10240], BF16),
        "vbf0": ([128, 10240], BF16), "lnr10": ([128, 10240], BF16),
        "modrow": ([4, D], F32), "pgin": ([512, D], BF16), "colm": ([128, 64], F32),
        "pre4": ([128, 8192], F32), "h2": ([128, 8192], F32),
        "pre3b": ([128, 8192], F32), "h4": ([128, 8192], F32),
        "qbfa": ([128, 8192], BF16), "qbfb": ([128, 8192], BF16),
        "ptld": ([128, 32], F32), "btot": ([128, 32], F32),
    }
    for t in taps:
        shape, dt = tapdefs[t]
        tap_t[t] = nc.dram_tensor("tap_" + t, shape, dt, kind="ExternalOutput")

    with ExitStack() as st:
        NP = 49920
        pool = st.enter_context(nc.sbuf_tensor("pool", [128, NP], F32))
        poolF = pool.ap()
        poolH = pool.ap().bitcast(BF16)

        def fview(off, width, p0=0, pn=128):
            return poolF[p0:p0 + pn, off:off + width]

        def hview(off_u, width_bf, p0=0, pn=128):
            return poolH[p0:p0 + pn, 2 * off_u: 2 * off_u + width_bf]

        O_XR, O_LNBF, O_QBF, O_SLAB = 0, 10240, 15360, 19456
        O_TS, O_ROWS, O_ATT, O_U = 27648, 30720, 32768, 33536

        XR = [fview(O_XR + i * 640, 640) for i in range(KC)]
        LNBF = [hview(O_LNBF + i * 320, 640) for i in range(KC)]
        QBF = [hview(O_QBF + i * 256, 512) for i in range(KC)]
        TS = [fview(O_TS + i * 512, 512) for i in range(6)]
        TS0BF = hview(O_TS, 512)
        S1 = fview(O_ATT, 256)
        S2 = fview(O_ATT + 256, 256)
        ATT = [hview(O_ATT + 512 + i * 128, 256) for i in range(2)]
        rows_a = fview(O_ROWS, 512, 0, 1)
        rows_b = fview(O_ROWS + 512, 512, 0, 1)
        rows_rstd = fview(O_ROWS + 1024, 512, 0, 1)
        rows2_bf = hview(O_ROWS + 1536, 512, 0, 2)
        onesf_row = fview(O_ROWS + 1792, 128, 0, 1)
        onesb_row = hview(O_ROWS + 1920, 128, 0, 1)
        U = O_U
        FW_T = [hview(U + j * 1024, 2048) for j in range(8)]
        PWB = [U + 8192, U + 8192 + 2048]
        FSTG = [hview(U + 12288 + j * 256, 512) for j in range(4)]
        BSTG = hview(U + 13312, 2048, 0, 1)
        PW0 = fview(U + 0, 2048)
        PW1 = fview(U + 2048, 2048)
        MODROW = [fview(U + 8192 + i * 2048, 2048, 0, 1) for i in range(2)]
        PSTG = [hview(U + 12288 + j * 1024, 2048) for j in range(2)]
        KBF = [hview(U + i * 320, 640) for i in range(KC)]
        VBF = [hview(U + 5120 + i * 1024, 2048) for i in range(5)]
        ARAW = [hview(U + 10240 + i * 256, 512) for i in range(KC)]
        HA = [fview(U + i * 512, 512) for i in range(KC)]
        HB = [fview(U + 8192 + i * 512, 512) for i in range(KC)]

        def smallt(name, shape, dtype=F32):
            return st.enter_context(nc.sbuf_tensor(name, shape, dtype))

        masks = smallt("masks_sb", [128, 3 * 128])
        colp = smallt("colp", [128, 64])
        colm = smallt("colm", [128, 64])
        colw = smallt("colw", [128, 64])
        col_ampb = smallt("col_ampb", [128, 16])
        col_phb = smallt("col_phb", [128, 16])
        col_entb = smallt("col_entb", [128, 16], BF16)
        col_entpb = smallt("col_entpb", [128, 32])
        col_btot = smallt("col_btot", [128, 32])
        col_ptld = smallt("col_ptld", [128, 32])
        col_pb = smallt("col_pb", [128, 64])
        col_g3 = smallt("col_g3", [128, 32])
        col_g4 = smallt("col_g4", [128, 32])
        col_b3 = smallt("col_b3", [128, 32])
        col_b4 = smallt("col_b4", [128, 32])
        ones_col_bf = smallt("ones_col_bf", [128, 1], BF16)
        pihalf_col = smallt("pihalf_col", [128, 1])
        col_m2 = smallt("col_m2", [128, 1])
        col_h05 = smallt("col_h05", [128, 1])
        eps1 = smallt("eps1", [1, 1])

        def mview(i):
            return masks.ap()[:, i * 128:(i + 1) * 128]

        PM = [st.enter_context(nc.psum_tensor(f"pm{i}", [128, 512], F32)) for i in range(4)]
        PA = [st.enter_context(nc.psum_tensor(f"pa{i}", [128, 512], F32)) for i in range(2)]
        PS = st.enter_context(nc.psum_tensor("ps", [128, 512], F32))
        PF = st.enter_context(nc.psum_tensor("pf", [128, 512], F32))
        PMPA = PM + PA

        semnames = ["s_pe", "s_act", "s_dve", "s_pool", "s_cc",
                    "d_sm", "d_sme", "d_mw", "d_fw", "d_pw0", "d_pw1",
                    "d_fst0", "d_fst1", "d_fst2", "d_fst3", "d_bst", "d_pf",
                    "d_x", "d_st", "d_tap", "d_sl0", "d_sl1", "d_sl0s", "d_sl1s",
                    "d_gb", "d_bt", "d_wc0", "d_wc1", "d_wc2", "d_wc3"]
        sems = {n: st.enter_context(nc.semaphore(n)) for n in semnames}
        S = Sched(sems)

        bank_free = {}
        slab_cons = {0: None, 1: None}
        ln_guard = {"p1": {0: None, 1: None}, "sq": None, "ts1": None, "ts2": None}

        def mm(ps_ap, lhsT_ap, rhs_ap, start, stop, sem=None):
            return S.op("tensor",
                        lambda e, o=ps_ap, l=lhsT_ap, r=rhs_ap, a=start, b=stop:
                        e.matmul(o, l, r, start=a, stop=b), sem=sem)

        def slab_load(slot, src2d, queue=None, n_kc=16, fwidth=512):
            queue = "sync"
            S.wait(queue, slab_cons[slot])
            src3 = src2d.rearrange("(kc p) f -> p kc f", p=128)
            dst3 = poolH[:, 2 * (O_SLAB + slot * 4096):
                         2 * (O_SLAB + slot * 4096) + n_kc * fwidth] \
                .rearrange("p (kc f) -> p kc f", f=fwidth)
            sem = f"d_sl{slot}s"
            h = n_kc // 2
            S.dma(queue, dst3[:, 0:h], src3[:, 0:h], sem)
            return S.dma(queue, dst3[:, h:n_kc], src3[:, h:n_kc], sem)

        def slab_tile(slot, kc, f0, fw, fwidth=512):
            base = 2 * (O_SLAB + slot * 4096) + kc * fwidth + f0
            return poolH[:, base:base + fw]

        def tap_sbuf(name, region_ap, mss):
            if name not in tap_t:
                return None
            S.waits("sync", mss)
            return S.dma("sync", tap_t[name].ap(), region_ap, "d_tap")

        def tap_dram(name, src_ap, mss):
            if name in tap_t:
                S.waits("sync", mss)
                S.dma("sync", tap_t[name].ap(), src_ap, "d_tap")

        def emit_ln(X, bf_tgt, fp_tgt, ttiles, gcol, bcol, pre_ms=(), pre_fc=None):
            last_ms = None
            fc_ms = [None] * KC
            for (c0, w) in ttiles:
                cast_fc = [None] * KC
                for fc in range(KC):
                    eng = "gpsimd" if fc % 2 == 0 else "vector"
                    S.waits(eng, pre_ms)
                    if pre_fc is not None:
                        S.wait(eng, pre_fc[fc])
                    dsta, srca = bf_tgt[fc][:, c0:c0 + w], X[fc][:, c0:c0 + w]
                    cast_fc[fc] = S.op(
                        eng, lambda e, d=dsta, s=srca: e.tensor_copy(d, s))
                S.wait("tensor", bank_free.get("ps"))
                S.wait("tensor", bank_free.get("pf"))
                S.wait("tensor", m_sets)
                S.wait("scalar", m_sets)
                prev_sq = ln_guard["sq"]
                for fc in range(KC):
                    S.waits("scalar", pre_ms)
                    if pre_fc is not None:
                        S.wait("scalar", pre_fc[fc])
                    S.wait("scalar", prev_sq)
                    dsta, srca = TS0BF[:, 0:w], X[fc][:, c0:c0 + w]
                    sq_ms = S.op("scalar",
                                 lambda e, d=dsta, s=srca: e.activation(d, s, AF.Square),
                                 sem="s_act")
                    S.wait("tensor", cast_fc[fc])
                    S.wait("tensor", sq_ms)
                    mm(PS.ap()[0:1, 0:w], ones_col_bf.ap(), bf_tgt[fc][:, c0:c0 + w],
                       start=(fc == 0), stop=(fc == KC - 1))
                    prev_sq = mm(PF.ap()[0:1, 0:w], ones_col_bf.ap(), TS0BF[:, 0:w],
                                 start=(fc == 0), stop=(fc == KC - 1),
                                 sem="s_pe")
                stats_pe = prev_sq
                ln_guard["sq"] = prev_sq
                S.wait("scalar", stats_pe)
                S.op("scalar", lambda e, w=w: e.activation(
                    rows_a[:, 0:w], PS.ap()[0:1, 0:w], AF.Square, scale=INVN))
                m_t1 = S.op("scalar", lambda e, w=w: e.activation(
                    rows_b[:, 0:w], PF.ap()[0:1, 0:w], AF.Copy, scale=INVN),
                    sem="s_act")
                S.wait("vector", m_t1)
                m_ve = S.op("vector", lambda e, w=w: e.tensor_tensor(
                    rows_b[:, 0:w], rows_b[:, 0:w], rows_a[:, 0:w], ALU.subtract),
                    sem="s_dve")
                S.wait("scalar", m_ve)
                S.op("scalar", lambda e, w=w: e.activation(
                    rows_a[:, 0:w], rows_b[:, 0:w], AF.Ln, bias=eps1.ap()))
                m_rstd = S.op("scalar", lambda e, w=w: e.activation(
                    rows_rstd[:, 0:w], rows_a[:, 0:w], AF.Exp, scale=-0.5), sem="s_act")
                m_mrow = S.op("scalar", lambda e, w=w: e.activation(
                    rows_a[:, 0:w], PS.ap()[0:1, 0:w], AF.Copy, scale=INVN),
                    sem="s_act")
                S.wait("tensor", m_rstd)
                m_bc = mm(PF.ap()[:, 0:w], onesf_row, rows_rstd[:, 0:w],
                          start=True, stop=True, sem="s_pe")
                S.wait("tensor", m_mrow)
                m_bc2 = mm(PS.ap()[:, 0:w], onesf_row, rows_a[:, 0:w],
                           start=True, stop=True, sem="s_pe")
                S.wait("scalar", m_bc)
                S.wait("scalar", ln_guard["ts1"])
                m_bce = S.op("scalar", lambda e, w=w: e.activation(
                    TS[1][:, 0:w], PF.ap()[:, 0:w], AF.Copy), sem="s_act")
                bank_free["pf"] = m_bce
                S.wait("scalar", m_bc2)
                S.wait("scalar", ln_guard["ts2"])
                m_bce2 = S.op("scalar", lambda e, w=w: e.activation(
                    TS[2][:, 0:w], PS.ap()[:, 0:w], AF.Copy), sem="s_act")
                bank_free["ps"] = m_bce2
                affine = gcol is not None
                for fc in range(KC):
                    S.wait("gpsimd", m_bce2)
                    S.wait("gpsimd", ln_guard["p1"][fc % 2])
                    t1d, t1s = TS[3 + fc % 2][:, 0:w], X[fc][:, c0:c0 + w]
                    m_p1 = S.op("gpsimd", lambda e, d=t1d, s=t1s, w=w:
                                e.tensor_tensor(d, s, TS[2][:, 0:w], ALU.subtract),
                                sem="s_pool")
                    ln_guard["ts2"] = m_p1
                    tgt = fp_tgt[fc][:, 0:w] if fp_tgt is not None \
                        else bf_tgt[fc][:, c0:c0 + w]
                    S.wait("vector", m_p1)
                    S.wait("vector", m_bce)
                    if affine:
                        m_t2 = S.op("vector", lambda e, d=t1d, w=w:
                                    e.tensor_tensor(d, d, TS[1][:, 0:w], ALU.mult),
                                    sem="s_dve")
                        ln_guard["ts1"] = m_t2
                        gsc = gcol[:, fc:fc + 1]
                        bsc = bcol[:, fc:fc + 1]
                        m_out = S.op("vector", lambda e, t=tgt, d=t1d, g=gsc, b=bsc:
                                     e.tensor_scalar(t, d, g, b, ALU.mult, ALU.add),
                                     sem="s_dve")
                    else:
                        m_out = S.op("vector", lambda e, t=tgt, d=t1d, w=w:
                                     e.tensor_tensor(t, d, TS[1][:, 0:w], ALU.mult),
                                     sem="s_dve")
                        ln_guard["ts1"] = m_out
                    ln_guard["p1"][fc % 2] = m_out
                    last_ms = m_out
                    if fp_tgt is not None and bf_tgt is not None:
                        S.wait("scalar", m_out)
                        cd, cs2 = bf_tgt[fc][:, c0:c0 + w], fp_tgt[fc][:, 0:w]
                        last_ms = S.op("scalar", lambda e, d=cd, s=cs2:
                                       e.activation(d, s, AF.Copy), sem="s_act")
                    fc_ms[fc] = last_ms
            return last_ms, fc_ms

        S.dma("sync", masks.ap().rearrange("p (m f) -> p m f", f=128),
              masks_in.ap().rearrange("m p f -> p m f"), "d_sm")
        S.dma("sync", col_ampb.ap(), ampb_col.ap(), "d_sm")
        S.dma("sync", col_phb.ap(), phb_col.ap(), "d_sm")
        S.dma("sync", col_entpb.ap().rearrange("p (l f) -> p l f", f=16),
              entpb_col.ap().rearrange("l p f -> p l f"), "d_sm")
        S.dma("sync", col_pb.ap().rearrange("p (l q f) -> p l q f", q=2, f=16),
              pathb_col.ap().rearrange("l q p f -> p l q f"), "d_sm")
        S.dma("sync", colp.ap().rearrange("p (l q f) -> p l q f", q=2, f=16),
              phase_col.ap().rearrange("l q p f -> p l q f"), "d_sm")
        S.dma("sync", col_g3.ap().rearrange("p (l f) -> p l f", f=16),
              ln3_gcol.ap().rearrange("l p f -> p l f"), "d_sm")
        S.dma("sync", col_g4.ap().rearrange("p (l f) -> p l f", f=16),
              ln4_gcol.ap().rearrange("l p f -> p l f"), "d_sm")
        S.dma("sync", col_b3.ap().rearrange("p (l f) -> p l f", f=16),
              ln3_bcol.ap().rearrange("l p f -> p l f"), "d_sm")
        m_small = S.dma("sync", col_b4.ap().rearrange("p (l f) -> p l f", f=16),
                        ln4_bcol.ap().rearrange("l p f -> p l f"), "d_sm")
        m_entb = S.dma("gpsimd", col_entb.ap().rearrange("p (l f) -> p l f", f=8),
                       entb_col.ap().rearrange("l p f -> p l f"), "d_sme")

        m_wcp = [None] * 24

        def emit_wcp(pjs):
            for pj in pjs:
                if pj < 12:
                    src = w_qkvT.ap()[:, pj * 512:(pj + 1) * 512]
                elif pj < 16:
                    src = w_oT.ap()[:, (pj - 12) * 512:(pj - 11) * 512]
                elif pj < 20:
                    src = w_amp.ap()[:, (pj - 16) * 512:(pj - 15) * 512]
                else:
                    src = w_ph.ap()[:, (pj - 20) * 512:(pj - 19) * 512]
                m_wcp[pj] = S.dma("gpsimd", wbf.ap()[:, pj * 512:(pj + 1) * 512],
                                  src, "d_wc0")
        S.op("vector", lambda e: e.memset(ones_col_bf.ap(), 1.0))
        S.op("vector", lambda e: e.memset(pihalf_col.ap(), PI / 2))
        S.op("vector", lambda e: e.memset(col_m2.ap(), -2.0))
        S.op("vector", lambda e: e.memset(col_h05.ap(), 0.5))
        S.op("vector", lambda e: e.memset(eps1.ap(), 1e-5))
        S.op("vector", lambda e: e.memset(onesf_row, 1.0))
        S.op("vector", lambda e: e.memset(onesb_row, 1.0))
        m_sets = S.op("vector", lambda e: e.memset(rows2_bf, 1.0), sem="s_dve")

        S.wait("vector", m_small)
        m_ampbh = S.op("vector", lambda e: e.tensor_scalar_mul(
            col_ampb.ap(), col_ampb.ap(), 0.5), sem="s_dve")

        S.wait("gpsimd", m_small)
        S.op("gpsimd", lambda e: e.tensor_scalar(colw.ap(), colp.ap(), PI, None, ALU.is_gt))
        S.op("gpsimd", lambda e: e.tensor_scalar_mul(colw.ap(), colw.ap(), -2 * PI))
        S.op("gpsimd", lambda e: e.tensor_tensor(colw.ap(), colw.ap(), colp.ap(), ALU.add))
        S.op("gpsimd", lambda e: e.tensor_scalar(colm.ap(), colw.ap(), -PI, None, ALU.is_lt))
        S.op("gpsimd", lambda e: e.tensor_scalar_mul(colm.ap(), colm.ap(), 2 * PI))
        m_wrap = S.op("gpsimd", lambda e: e.tensor_tensor(
            colw.ap(), colm.ap(), colw.ap(), ALU.add), sem="s_pool")
        S.wait("scalar", m_wrap)
        S.wait("scalar", m_sets)
        m_sin = S.op("scalar", lambda e: e.activation(colm.ap(), colw.ap(), AF.Sin),
                     sem="s_act")
        S.wait("gpsimd", m_sin)
        S.op("gpsimd", lambda e: e.tensor_scalar(colp.ap(), colw.ap(), PI / 2, None,
                                                 ALU.is_gt))
        S.op("gpsimd", lambda e: e.tensor_scalar_mul(colp.ap(), colp.ap(), -2 * PI))
        m_wr2 = S.op("gpsimd", lambda e: e.tensor_tensor(
            colw.ap(), colp.ap(), colw.ap(), ALU.add), sem="s_pool")
        S.wait("scalar", m_wr2)
        m_cos = S.op("scalar", lambda e: e.activation(colp.ap(), colw.ap(), AF.Sin,
                                                      bias=pihalf_col.ap()), sem="s_act")
        S.wait("vector", m_cos)
        m_mod = S.op("vector", lambda e: e.tensor_tensor(colm.ap(), colm.ap(), colp.ap(),
                                                         ALU.add), sem="s_dve")
        S.wait("gpsimd", m_mod)
        for l in range(2):
            S.op("gpsimd", lambda e, l=l: e.tensor_tensor(
                colw.ap()[:, 0:16], col_pb.ap()[:, (2 * l) * 16:(2 * l + 1) * 16],
                colm.ap()[:, (2 * l) * 16:(2 * l + 1) * 16], ALU.mult))
            S.op("gpsimd", lambda e, l=l: e.tensor_tensor(
                colw.ap()[:, 16:32], col_pb.ap()[:, (2 * l + 1) * 16:(2 * l + 2) * 16],
                colm.ap()[:, (2 * l + 1) * 16:(2 * l + 2) * 16], ALU.mult))
            S.op("gpsimd", lambda e, l=l: e.tensor_tensor(
                col_ptld.ap()[:, l * 16:(l + 1) * 16], colw.ap()[:, 0:16],
                colw.ap()[:, 16:32], ALU.add))
        m_mr_st = []
        for l in range(2):
            for p in range(2):
                S.wait("gpsimd", m_mod)
                m_mr_st.append(S.dma(
                    "gpsimd",
                    modrow_d.ap()[2 * l + p, :].rearrange("(a p) -> p a", p=128),
                    colm.ap()[:, (2 * l + p) * 16:(2 * l + p) * 16 + 16],
                    "d_mw", allow_slow_non_contiguous=True))

        pw_cons = [None, None]
        stg_free = [None] * 4
        bstg_free = None
        fold_mm_last = None
        fold_store_last = None
        for l in range(2):
            wt_ms = []
            S.wait("gpsimd", fold_mm_last)
            for j in range(8):
                wt_ms.append(S.dma("gpsimd", FW_T[j][:, :],
                                   entwT_sl.ap()[l, j * 128:(j + 1) * 128, :], "d_fw"))
            m_bev = None
            for fs in range(4):
                slot = fs % 2
                S.wait("gpsimd", pw_cons[slot])
                pwdst = poolH[:, 2 * PWB[slot]: 2 * PWB[slot] + 8 * 512] \
                    .rearrange("p (jc f) -> p jc f", f=512)
                pwsrc = entpw_sl.ap()[l, :, fs * 512:(fs + 1) * 512] \
                    .rearrange("(jc p) f -> p jc f", p=128)
                m_pw = S.dma("gpsimd", pwdst, pwsrc, f"d_pw{slot}")
                emit_wcp([2 * fs, 2 * fs + 1] if l == 0 else [8 + fs])
                S.waits("tensor", wt_ms)
                S.wait("tensor", m_pw)
                S.wait("tensor", m_entb)

                def pwtile(j, slot=slot):
                    base = 2 * PWB[slot] + j * 512
                    return poolH[:, base:base + 512]

                S.wait("tensor", bank_free.get("pf"))
                m_b = None
                for j in range(8):
                    m_b = mm(PF.ap()[0:1, 0:512],
                             col_entb.ap()[:, l * 8 + j:l * 8 + j + 1], pwtile(j),
                             start=(j == 0), stop=(j == 7),
                             sem="s_pe" if j == 7 else None)
                S.wait("scalar", m_b)
                S.wait("scalar", bstg_free)
                m_bev = S.op("scalar", lambda e, fs=fs: e.activation(
                    BSTG[:, fs * 512:(fs + 1) * 512], PF.ap()[0:1, 0:512], AF.Copy),
                    sem="s_act")
                bank_free["pf"] = m_bev
                for dc in range(KC):
                    bk = PM[dc % 2]
                    S.wait("tensor", bank_free.get(bk.name))
                    m_mmv = None
                    for j in range(8):
                        m_mmv = mm(bk.ap()[:, 0:512],
                                   FW_T[j][:, dc * 128:(dc + 1) * 128], pwtile(j),
                                   start=(j == 0), stop=(j == 7),
                                   sem="s_pe" if j == 7 else None)
                    fold_mm_last = m_mmv
                    sg = dc % 4
                    S.wait("scalar", stg_free[sg])
                    S.wait("scalar", m_mmv)
                    m_ev = S.op("scalar", lambda e, sg=sg, dc=dc: e.activation(
                        FSTG[sg][:, 0:512], PM[dc % 2].ap()[:, 0:512], AF.Copy),
                        sem="s_act")
                    bank_free[bk.name] = m_ev
                    S.wait("sync", m_ev)
                    m_stx = S.dma("sync",
                                  fold_in.ap()[l * FOLD_LROWS + dc * 128:
                                               l * FOLD_LROWS + (dc + 1) * 128,
                                               fs * 512:(fs + 1) * 512],
                                  FSTG[sg][:, 0:512], f"d_fst{sg}")
                    stg_free[sg] = m_stx
                    fold_store_last = m_stx
                pw_cons[slot] = fold_mm_last
            S.wait("sync", m_bev)
            S.wait("sync", bstg_free)
            fold_store_last = S.dma(
                "sync",
                fold_in.ap()[l * FOLD_LROWS + 2048: l * FOLD_LROWS + 2049, :],
                BSTG, "d_bst")
            bstg_free = fold_store_last

        emit_wcp(range(12, 24))
        S.waits("gpsimd", [S.mark("d_fst0"), S.mark("d_fst1"), S.mark("d_fst2"),
                           S.mark("d_fst3"), S.mark("d_bst")])
        m_ar = S.op("gpsimd", lambda e: e.collective_compute(
            "AllReduce", ALU.add, replica_groups=[list(range(NCORES))],
            ins=[fold_in.ap().opt()], outs=[fold_out.ap().opt()]), sem="s_cc")

        pf_state = {}

        def emit_pfold():
            pfold_dve_last = None
            pstg_store = [None, None]
            for l in range(2):
                mr_ld = []
                for p in range(2):
                    S.wait("sync", fold_mm_last)
                    S.wait("sync", S.mark("s_pe"))
                    S.waits("sync", m_mr_st)
                    mr_ld.append(S.dma("sync", MODROW[p], modrow_d.ap()[2 * l + p:2 * l + p + 1, :],
                                       "d_pf"))
                for rt in range(2):
                    S.wait("sync", fold_mm_last)
                    S.wait("sync", pfold_dve_last)
                    m_pw0 = S.dma("sync", PW0, pathw_sl.ap()[l, 0, rt * 128:(rt + 1) * 128, :],
                                  "d_pf")
                    m_pw1 = S.dma("sync", PW1, pathw_sl.ap()[l, 1, rt * 128:(rt + 1) * 128, :],
                                  "d_pf")
                    m_pf_all = S.mark("d_pf")
                    for fs in range(4):
                        S.wait("tensor", bank_free.get("pf"))
                        S.wait("tensor", m_pf_all)
                        m_k1 = mm(PF.ap()[:, 0:512], onesf_row,
                                  MODROW[0][:, fs * 512:(fs + 1) * 512],
                                  start=True, stop=True, sem="s_pe")
                        S.wait("vector", m_k1)
                        S.wait("vector", m_pf_all)
                        m_t1 = S.op("vector", lambda e, fs=fs: e.tensor_tensor(
                            TS[4][:, 0:512], PW0[:, fs * 512:(fs + 1) * 512],
                            PF.ap()[:, 0:512], ALU.mult), sem="s_dve")
                        bank_free["pf"] = m_t1
                        S.wait("tensor", m_t1)
                        m_k2 = mm(PF.ap()[:, 0:512], onesf_row,
                                  MODROW[1][:, fs * 512:(fs + 1) * 512],
                                  start=True, stop=True, sem="s_pe")
                        S.wait("vector", m_k2)
                        m_t2 = S.op("vector", lambda e, fs=fs: e.tensor_tensor(
                            TS[5][:, 0:512], PW1[:, fs * 512:(fs + 1) * 512],
                            PF.ap()[:, 0:512], ALU.mult), sem="s_dve")
                        bank_free["pf"] = m_t2
                        S.waits("vector", stg_free + [bstg_free, pstg_store[rt]])
                        pfold_dve_last = S.op("vector", lambda e, rt=rt, fs=fs: e.tensor_tensor(
                            PSTG[rt][:, fs * 512:(fs + 1) * 512], TS[4][:, 0:512],
                            TS[5][:, 0:512], ALU.add), sem="s_dve")
                    S.wait("sync", pfold_dve_last)
                    pstg_store[rt] = S.dma(
                        "sync", pg_in.ap()[l * 256 + rt * 128: l * 256 + (rt + 1) * 128, :],
                        PSTG[rt], f"d_fst{rt}")
            S.wait("gpsimd", pstg_store[0])
            S.wait("gpsimd", pstg_store[1])
            m_ag = S.op("gpsimd", lambda e: e.collective_compute(
                "AllGather", ALU.bypass, replica_groups=[list(range(NCORES))],
                ins=[pg_in.ap().opt()], outs=[pg_out.ap().opt()]), sem="s_cc")

            pf_state["dve"] = pfold_dve_last
            pf_state["st"] = [pstg_store[0], pstg_store[1]]

        tap_dram("mfold", fold_out.ap(), [S.mark("s_cc")])
        tap_dram("pg", pg_out.ap(), [S.mark("s_cc")])
        tap_dram("modrow", modrow_d.ap(), [S.mark("s_cc")])
        tap_dram("pgin", pg_in.ap(), [S.mark("s_cc")])
        tap_sbuf("colm", colm.ap(), [S.mark("s_cc")])

        tap_sbuf("ptld", col_ptld.ap(), [S.mark("s_dve")])
        m_Ufree = (fold_mm_last,)

        m_r1_store = [None] * NCH
        m_x_ld = None
        prev_attn_pe = None
        prev_qkv_pe = None
        attn_guard = {"relu3": None, "att3": None, "out2": None, "out3": None}
        m_aend = None

        for c in range(NCH):
            if c == 0:
                toff, wch, gc0 = 0, 640, 0
                ln_tt = [(0, 512), (512, 128)]
                kv_tt = [(0, 512), (512, 128)]
                v_tps = [0, 1, 2, 3, 4]
            else:
                toff, wch, gc0 = 128, 512, 128 + c * 512
                ln_tt = [(128, 512)]
                kv_tt = [(128, 512)]
                v_tps = [1, 2, 3, 4]

            S.wait("sync", m_r1_store[c - 1] if c > 0 else None)
            xdst = poolF[:, O_XR:O_XR + 10240] \
                .rearrange("p (fc t) -> p fc t", t=640)[:, :, toff:toff + wch]
            xsrc = x_fm.ap()[:, gc0:gc0 + wch].rearrange("(fc p) t -> p fc t", p=128)
            m_x_a = S.dma("sync", xdst[:, 0:8], xsrc[:, 0:8], "d_x")
            m_x_ld = S.dma("sync", xdst[:, 8:16], xsrc[:, 8:16], "d_x")

            m_tail = None
            if c > 0:
                S.wait("gpsimd", prev_attn_pe)
                for fc in range(KC):
                    S.op("gpsimd", lambda e, fc=fc: e.tensor_copy(
                        KBF[fc][:, 0:128], KBF[fc][:, 512:640]))
                m_tail = S.op("gpsimd", lambda e: e.tensor_copy(
                    VBF[0][:, :], VBF[4][:, :]), sem="s_pool")

            ln_pre = [prev_qkv_pe]
            x_fcms = [m_x_a] * 8 + [m_x_ld] * 8
            m_lnx, m_lnx_fc = emit_ln(XR, LNBF, None, ln_tt, None, None,
                                      pre_ms=ln_pre, pre_fc=x_fcms)
            if c == 0:
                tap_sbuf("lnx0", poolH[:, 2 * O_LNBF: 2 * O_LNBF + 10240], [m_lnx])

            m_qkv_ev = []
            for fg in range(12):
                slot = fg % 2
                S.wait("sync", m_wcp[fg])
                m_slab = slab_load(slot, wbf.ap()[:, fg * 512:(fg + 1) * 512])
                S.wait("tensor", m_slab)
                S.wait("tensor", m_lnx)
                if fg < 8:
                    is_q = fg < 4
                    for fcg in range(4):
                        fc = (fg % 4) * 4 + fcg
                        bk = PMPA[(fg * 4 + fcg) % 6]
                        tts = [(128, 512)] if is_q else kv_tt
                        for (t0, tw) in tts:
                            S.wait("tensor", bank_free.get(bk.name))
                            m_g = None
                            for kc in range(KC):
                                S.wait("tensor", m_lnx_fc[kc])
                                m_g = mm(bk.ap()[:, 0:tw],
                                         slab_tile(slot, kc, fcg * 128, 128),
                                         LNBF[kc][:, t0:t0 + tw],
                                         start=(kc == 0), stop=(kc == KC - 1),
                                         sem="s_pe" if kc == KC - 1 else None)
                            if is_q:
                                dst = QBF[fc][:, t0 - 128:t0 - 128 + tw]
                            else:
                                dst = KBF[fc][:, t0:t0 + tw]
                                S.waits("scalar", m_Ufree)
                                if m_tail:
                                    S.wait("scalar", m_tail)
                            S.wait("scalar", m_g)
                            m_e = S.op("scalar", lambda e, d=dst, b=bk, tw=tw:
                                       e.activation(d, b.ap()[:, 0:tw], AF.Copy),
                                       sem="s_act")
                            bank_free[bk.name] = m_e
                            m_qkv_ev.append(m_e)
                else:
                    vs = fg - 8
                    for tp in v_tps:
                        bk = PM[tp % 2]
                        S.wait("tensor", bank_free.get(bk.name))
                        m_g = None
                        for kc in range(KC):
                            S.wait("tensor", m_lnx_fc[kc])
                            m_g = mm(bk.ap()[:, 0:512],
                                     LNBF[kc][:, tp * 128:(tp + 1) * 128],
                                     slab_tile(slot, kc, 0, 512),
                                     start=(kc == 0), stop=(kc == KC - 1),
                                     sem="s_pe" if kc == KC - 1 else None)
                        S.waits("scalar", m_Ufree)
                        if m_tail:
                            S.wait("scalar", m_tail)
                        S.wait("scalar", m_g)
                        m_e = S.op("scalar", lambda e, tp=tp, vs=vs, b=bk: e.activation(
                            VBF[tp][:, vs * 512:(vs + 1) * 512], b.ap()[:, 0:512],
                            AF.Copy), sem="s_act")
                        bank_free[bk.name] = m_e
                        m_qkv_ev.append(m_e)
                slab_cons[slot] = S.mark("s_pe")
            prev_qkv_pe = S.mark("s_pe")
            if c == 0:
                tap_sbuf("qbf0", poolH[:, 2 * O_QBF: 2 * O_QBF + 8192], m_qkv_ev[-1:])
                tap_sbuf("kbf0", poolH[:, 2 * U: 2 * U + 10240], m_qkv_ev[-1:])
                tap_sbuf("vbf0", poolH[:, 2 * (U + 5120): 2 * (U + 5120) + 10240],
                         m_qkv_ev[-1:])

            m_sim = [None] * 4
            m_maskread = [None] * 4
            m_att = [None] * 4
            m_outpe = [None] * 4
            m_relu = [None] * 4

            def emit_sims(j):
                bk = PA[j % 2]
                S.wait("tensor", bank_free.get(bk.name))
                S.waits("tensor", m_qkv_ev)
                last = None
                for dc in range(KC):
                    mm(bk.ap()[:, 0:128], KBF[dc][:, j * 128:j * 128 + 128],
                       QBF[dc][:, j * 128:(j + 1) * 128],
                       start=(dc == 0), stop=(dc == KC - 1))
                for dc in range(KC):
                    last = mm(bk.ap()[:, 128:256],
                              KBF[dc][:, 128 + j * 128:128 + j * 128 + 128],
                              QBF[dc][:, j * 128:(j + 1) * 128],
                              start=(dc == 0), stop=(dc == KC - 1),
                              sem="s_pe" if dc == KC - 1 else None)
                m_sim[j] = last

            def emit_attn_ew(j):
                bk = PA[j % 2]
                mprev = mview(0) if (c == 0 and j == 0) else mview(1)
                S.wait("vector", m_small)
                S.wait("vector", m_sim[j])
                if j >= 1:
                    S.wait("vector", m_relu[j - 1])
                else:
                    S.wait("vector", attn_guard["relu3"])
                S.op("vector", lambda e, b=bk, m=mprev: e.tensor_tensor(
                    S1[:, 0:128], b.ap()[:, 0:128], m, ALU.mult))
                m_mr = S.op("vector", lambda e, b=bk: e.tensor_tensor(
                    S1[:, 128:256], b.ap()[:, 128:256], mview(2), ALU.mult),
                    sem="s_dve")
                m_maskread[j] = m_mr
                bank_free[bk.name] = m_mr
                S.wait("scalar", m_mr)
                if j >= 1:
                    S.wait("scalar", m_att[j - 1])
                else:
                    S.wait("scalar", attn_guard["att3"])
                m_rl = S.op("scalar", lambda e: e.activation(
                    S2[:, 0:256], S1[:, 0:256], AF.Relu, scale=1.0 / 128.0),
                    sem="s_act")
                m_relu[j] = m_rl
                S.wait("vector", m_rl)
                if j >= 2:
                    S.wait("vector", m_outpe[j - 2])
                else:
                    S.wait("vector", attn_guard["out2"] if j == 0 else attn_guard["out3"])
                m_sq = S.op("vector", lambda e, j=j: e.tensor_tensor(
                    ATT[j % 2][:, 0:256], S2[:, 0:256], S2[:, 0:256], ALU.mult),
                    sem="s_dve")
                m_att[j] = m_sq

            def emit_outs(j):
                S.wait("tensor", m_att[j])
                last = None
                for dc in range(KC):
                    bk = PM[dc % 2]
                    S.wait("tensor", bank_free.get(bk.name))
                    mm(bk.ap()[:, 0:128], VBF[j][:, dc * 128:(dc + 1) * 128],
                       ATT[j % 2][:, 0:128], start=True, stop=False)
                    last = mm(bk.ap()[:, 0:128], VBF[j + 1][:, dc * 128:(dc + 1) * 128],
                              ATT[j % 2][:, 128:256], start=False, stop=True,
                              sem="s_pe")
                    S.wait("scalar", last)
                    m_e = S.op("scalar", lambda e, dc=dc, j=j, b=bk: e.activation(
                        ARAW[dc][:, j * 128:(j + 1) * 128], b.ap()[:, 0:128], AF.Copy),
                        sem="s_act")
                    bank_free[bk.name] = m_e
                m_outpe[j] = last

            emit_sims(0)
            emit_attn_ew(0)
            emit_sims(1)
            emit_attn_ew(1)
            emit_outs(0)
            emit_sims(2)
            emit_attn_ew(2)
            emit_outs(1)
            emit_sims(3)
            emit_attn_ew(3)
            emit_outs(2)
            emit_outs(3)
            attn_guard["relu3"] = m_relu[3]
            attn_guard["att3"] = m_att[3]
            attn_guard["out2"] = m_outpe[2]
            attn_guard["out3"] = m_outpe[3]
            m_att_act = S.mark("s_act")
            if c == 0:
                tap_sbuf("araw0", poolH[:, 2 * (U + 10240): 2 * (U + 10240) + 8192],
                         [m_att_act])

            m_r1 = []
            for fg in range(4):
                slot = fg % 2
                S.wait("sync", m_wcp[12 + fg])
                m_slab = slab_load(slot, wbf.ap()[:, 3 * D + fg * 512:
                                                  3 * D + (fg + 1) * 512])
                S.wait("tensor", m_slab)
                S.wait("tensor", m_att_act)
                for fcg in range(4):
                    fc = fg * 4 + fcg
                    bk = PMPA[(fg * 4 + fcg) % 6]
                    S.wait("tensor", bank_free.get(bk.name))
                    m_g = None
                    for kc in range(KC):
                        m_g = mm(bk.ap()[:, 0:512],
                                 slab_tile(slot, kc, fcg * 128, 128),
                                 ARAW[kc][:, 0:512],
                                 start=(kc == 0), stop=(kc == KC - 1),
                                 sem="s_pe" if kc == KC - 1 else None)
                    S.wait("vector", m_g)
                    S.wait("vector", m_x_ld)
                    m_add = S.op("vector", lambda e, fc=fc, b=bk: e.tensor_tensor(
                        XR[fc][:, 128:640], b.ap()[:, 0:512], XR[fc][:, 128:640],
                        ALU.add), sem="s_dve")
                    bank_free[bk.name] = m_add
                    m_r1.append(m_add)
                slab_cons[slot] = S.mark("s_pe")
            r1dst = r1s.ap()[:, c * 512:(c + 1) * 512] \
                .rearrange("(fc p) t -> p fc t", p=128)
            r1src = poolF[:, O_XR:O_XR + 10240] \
                .rearrange("p (fc t) -> p fc t", t=640)[:, :, 128:640]
            S.wait("sync", m_r1[7])
            S.dma("sync", r1dst[:, 0:8], r1src[:, 0:8], "d_st")
            S.wait("sync", m_r1[15])
            m_r1_store[c] = S.dma("sync", r1dst[:, 8:16], r1src[:, 8:16], "d_st")
            prev_attn_pe = S.mark("s_pe")

        m_aend = S.mark("s_pe")
        emit_pfold()
        m_b_gate = [m_aend, pf_state["dve"]] + pf_state["st"]
        tap_dram("r1", r1s.ap(), [m_r1_store[NCH - 1]])

        S.wait("gpsimd", ("s_cc", 1))
        m_bt = []
        for l in range(2):
            m_bt.append(S.dma(
                "gpsimd", col_btot.ap()[:, l * 16:(l + 1) * 16],
                fold_out.ap()[l * FOLD_LROWS + 2048, :].rearrange("(a p) -> p a", p=128),
                "d_bt", allow_slow_non_contiguous=True))
        S.waits("gpsimd", m_bt)
        S.wait("gpsimd", m_small)
        m_btot = S.op("gpsimd", lambda e: e.tensor_tensor(
            col_btot.ap(), col_entpb.ap(), col_btot.ap(), ALU.add), sem="s_pool")
        tap_sbuf("btot", col_btot.ap(), [m_btot])

        m_out_store = None
        for c in range(NCH):
            S.wait("sync", m_out_store)
            S.wait("sync", m_r1_store[NCH - 1])
            r1src = r1s.ap()[:, c * 512:(c + 1) * 512] \
                .rearrange("(fc p) t -> p fc t", p=128)
            xdst = poolF[:, O_XR:O_XR + 10240] \
                .rearrange("p (fc t) -> p fc t", t=640)[:, :, 128:640]
            m_r1ld_a = S.dma("sync", xdst[:, 0:8], r1src[:, 0:8], "d_x")
            m_r1ld = S.dma("sync", xdst[:, 8:16], r1src[:, 8:16], "d_x")

            r1_fcms = [m_r1ld_a] * 8 + [m_r1ld] * 8
            m_ln2, m_ln2_fc = emit_ln(XR, LNBF, None, [(128, 512)], None, None,
                                      pre_ms=[m_aend], pre_fc=r1_fcms)
            if c == 0:
                tap_sbuf("lnr10", poolH[:, 2 * O_LNBF: 2 * O_LNBF + 10240], [m_ln2])

            m_amp_ev = [None] * KC
            m_ph_ev = [None] * KC
            for fg in range(8):
                slot = fg % 2
                wbase = 4 * D if fg < 4 else 5 * D
                S.wait("sync", m_wcp[(16 if fg < 4 else 20) + (fg % 4)])
                m_slab = slab_load(slot, wbf.ap()[:, wbase + (fg % 4) * 512:
                                                  wbase + (fg % 4 + 1) * 512])
                S.wait("tensor", m_slab)
                for fcg in range(4):
                    fc = (fg % 4) * 4 + fcg
                    bk = PMPA[(fg * 4 + fcg) % 6]
                    S.wait("tensor", bank_free.get(bk.name))
                    m_g = None
                    for kc in range(KC):
                        S.wait("tensor", m_ln2_fc[kc])
                        m_g = mm(bk.ap()[:, 0:512],
                                 slab_tile(slot, kc, fcg * 128, 128),
                                 LNBF[kc][:, 128:640],
                                 start=(kc == 0), stop=(kc == KC - 1),
                                 sem="s_pe" if kc == KC - 1 else None)
                    S.wait("scalar", m_g)
                    S.waits("scalar", m_b_gate + [m_ampbh, m_small])
                    if fg < 4:
                        m_e = S.op("scalar", lambda e, fc=fc, b=bk: e.activation(
                            HA[fc][:, :], b.ap()[:, 0:512], AF.Tanh, scale=0.5,
                            bias=col_ampb.ap()[:, fc:fc + 1]), sem="s_act")
                        m_amp_ev[fc] = m_e
                    else:
                        m_e = S.op("scalar", lambda e, fc=fc, b=bk: e.activation(
                            HB[fc][:, :], b.ap()[:, 0:512], AF.Tanh,
                            bias=col_phb.ap()[:, fc:fc + 1]), sem="s_act")
                        m_ph_ev[fc] = m_e
                    bank_free[bk.name] = m_e
                slab_cons[slot] = S.mark("s_pe")

            m_chain_prev = None
            m_cos_prev = None
            m_cs_prev = None
            m_h0cast = []
            for fc in range(KC):
                S.wait("vector", m_ph_ev[fc])
                S.wait("vector", m_cos_prev)
                S.op("vector", lambda e, fc=fc: e.tensor_scalar(
                    TS[2][:, :], HB[fc][:, :], 0.5, None, ALU.is_gt))
                m_wr = S.op("vector", lambda e, fc=fc: e.scalar_tensor_tensor(
                    TS[2][:, :], TS[2][:, :], col_m2.ap(), HB[fc][:, :],
                    ALU.mult, ALU.add), sem="s_dve")
                S.wait("scalar", m_cs_prev)
                m_s = S.op("scalar", lambda e, fc=fc: e.activation(
                    TS[3][:, :], HB[fc][:, :], AF.Sin, scale=PI), sem="s_act")
                S.wait("scalar", m_wr)
                S.wait("scalar", m_sets)
                m_cos = S.op("scalar", lambda e: e.activation(
                    TS[4][:, :], TS[2][:, :], AF.Sin, scale=PI,
                    bias=pihalf_col.ap()), sem="s_act")
                m_cos_prev = m_cos
                m_av = S.op("scalar", lambda e, fc=fc: e.activation(
                    TS[2][:, :], HA[fc][:, :], AF.Copy, scale=0.5,
                    bias=0.5), sem="s_act")
                S.wait("vector", m_cos)
                m_cs = S.op("vector", lambda e: e.tensor_tensor(
                    TS[5][:, :], TS[3][:, :], TS[4][:, :], ALU.add), sem="s_dve")
                m_cs_prev = m_cs
                S.wait("vector", m_av)
                m_h0 = S.op("vector", lambda e, fc=fc: e.tensor_tensor(
                    HA[fc][:, :], TS[5][:, :], TS[2][:, :], ALU.mult), sem="s_dve")
                m_chain_prev = m_h0
                S.wait("scalar", m_h0)
                m_cc = S.op("scalar", lambda e, fc=fc: e.activation(
                    QBF[fc][:, :], HA[fc][:, :], AF.Copy), sem="s_act")
                m_h0cast.append(m_cc)
            if c == 0:
                tms = tap_sbuf("h0", poolF[:, U:U + 8192], [m_chain_prev])
                if tms:
                    m_h0cast = m_h0cast + [tms]

            hcur, hnxt = HA, HB
            m_hbf = list(m_h0cast[:KC])
            m_hcur = [m_chain_prev]
            for l in range(2):
                m_pre = []
                for fg in range(4):
                    slot = fg % 2
                    S.wait("sync", ("s_cc", 1))
                    m_slab = slab_load(
                        slot,
                        fold_out.ap()[l * FOLD_LROWS: l * FOLD_LROWS + 2048,
                                      fg * 512:(fg + 1) * 512])
                    S.wait("tensor", m_slab)
                    for fcg in range(4):
                        fc = fg * 4 + fcg
                        bk = PMPA[(fg * 4 + fcg) % 6]
                        S.wait("tensor", bank_free.get(bk.name))
                        m_g = None
                        for kc in range(KC):
                            S.wait("tensor", m_hbf[kc])
                            m_g = mm(bk.ap()[:, 0:512],
                                     slab_tile(slot, kc, fcg * 128, 128),
                                     QBF[kc][:, :],
                                     start=(kc == 0), stop=(kc == KC - 1),
                                     sem="s_pe" if kc == KC - 1 else None)
                        S.wait("vector", m_g)
                        S.wait("vector", m_btot)
                        S.waits("vector", m_hcur)
                        m_p = S.op("vector", lambda e, fc=fc, l=l, b=bk, hc=hcur, hn=hnxt:
                                   e.scalar_tensor_tensor(
                                       hn[fc][:, :], b.ap()[:, 0:512],
                                       col_btot.ap()[:, l * 16 + fc:l * 16 + fc + 1],
                                       hc[fc][:, :], ALU.add, ALU.add), sem="s_dve")
                        bank_free[bk.name] = m_p
                        m_pre.append(m_p)
                    slab_cons[slot] = S.mark("s_pe")
                if c == 0 and l == 0:
                    tms = tap_sbuf("pre3", poolF[:, U + 8192:U + 16384], m_pre[-1:])
                    if tms:
                        m_pre = m_pre + [tms]
                hcur, hnxt = hnxt, hcur
                m_ln3, m_ln3_fc = emit_ln(hcur, QBF, hnxt, [(0, 512)],
                                          col_g3.ap()[:, l * 16:(l + 1) * 16],
                                          col_b3.ap()[:, l * 16:(l + 1) * 16],
                                          pre_ms=m_pre)
                hcur, hnxt = hnxt, hcur
                m_hbf = list(m_ln3_fc)
                m_hcur = [m_ln3]
                if c == 0 and l == 0:
                    tms = tap_sbuf("h1", poolF[:, U:U + 8192], [m_ln3])
                    if tms:
                        m_hbf = m_hbf + [tms]
                        m_hcur = m_hcur + [tms]
                if c == 0 and l == 1:
                    tms = tap_sbuf("pre3b", poolF[:, U + 8192:U + 16384], m_pre[-1:])
                    if tms:
                        m_pre = m_pre + [tms]
                if c == 0 and l == 0:
                    tq = tap_sbuf("qbfa", poolH[:, 2 * O_QBF: 2 * O_QBF + 8192], m_hbf)
                    if tq:
                        m_hbf = m_hbf + [tq]
                m_pre4 = []
                for fg in range(4):
                    slot = fg % 2
                    pq = "sync"
                    S.wait(pq, ("s_cc", 2))
                    S.wait(pq, slab_cons[slot])
                    m_slab = None
                    for h in range(2):
                        for rr in range(8):
                            row0 = rr * 512 + l * 256 + h * 128
                            src = bass.AP(pg_out, row0 * D + fg * 512,
                                          [[D, 128], [1, 512]])
                            m_slab = S.dma(pq, slab_tile(slot, h * 8 + rr, 0, 512),
                                           src, f"d_sl{slot}s")
                    S.wait("tensor", m_slab)
                    for fcg in range(4):
                        fc = fg * 4 + fcg
                        bk = PMPA[(fg * 4 + fcg) % 6]
                        S.wait("tensor", bank_free.get(bk.name))
                        m_g = None
                        for kc in range(KC):
                            pos = (kc % 2) * 8 + kc // 2
                            S.wait("tensor", m_hbf[kc])
                            m_g = mm(bk.ap()[:, 0:512],
                                     slab_tile(slot, pos, fcg * 128, 128),
                                     QBF[kc][:, :],
                                     start=(kc == 0), stop=(kc == KC - 1),
                                     sem="s_pe" if kc == KC - 1 else None)
                        S.wait("vector", m_g)
                        S.waits("vector", m_hcur)
                        m_p = S.op("vector", lambda e, fc=fc, l=l, b=bk, hc=hcur, hn=hnxt:
                                   e.scalar_tensor_tensor(
                                       hn[fc][:, :], b.ap()[:, 0:512],
                                       col_ptld.ap()[:, l * 16 + fc:l * 16 + fc + 1],
                                       hc[fc][:, :], ALU.add, ALU.add), sem="s_dve")
                        bank_free[bk.name] = m_p
                        m_pre4.append(m_p)
                    slab_cons[slot] = S.mark("s_pe")
                if c == 0 and l == 0:
                    tq2 = tap_sbuf("qbfb", poolH[:, 2 * O_QBF: 2 * O_QBF + 8192],
                                   m_pre4[-1:])
                    if tq2:
                        m_pre4 = m_pre4 + [tq2]
                hcur, hnxt = hnxt, hcur
                m_ln4, m_ln4_fc = emit_ln(hcur, QBF, hnxt, [(0, 512)],
                                          col_g4.ap()[:, l * 16:(l + 1) * 16],
                                          col_b4.ap()[:, l * 16:(l + 1) * 16],
                                          pre_ms=m_pre4)
                hcur, hnxt = hnxt, hcur
                m_hbf = list(m_ln4_fc)
                m_hcur = [m_ln4]
                if c == 0 and l == 0:
                    t1 = tap_sbuf("pre4", poolF[:, U + 8192:U + 16384], m_pre4[-1:])
                    t2 = tap_sbuf("h2", poolF[:, U:U + 8192], [m_ln4])
                    for t in (t1, t2):
                        if t:
                            m_hbf = m_hbf + [t]
                            m_hcur = m_hcur + [t]
                if c == 0 and l == 1:
                    t3 = tap_sbuf("h4", poolF[:, U:U + 8192], [m_ln4])
                    if t3:
                        m_hcur = m_hcur + [t3]

            m_fin = []
            for fc in range(KC):
                S.wait("vector", m_ln4_fc[fc])
                S.waits("vector", m_hcur[1:])
                S.wait("vector", m_r1ld)
                m_f = S.op("vector", lambda e, fc=fc, hc=hcur: e.tensor_tensor(
                    XR[fc][:, 128:640], hc[fc][:, :], XR[fc][:, 128:640], ALU.add),
                    sem="s_dve")
                m_fin.append(m_f)
            odst = out_fm.ap()[:, c * 512:(c + 1) * 512] \
                .rearrange("(fc p) t -> p fc t", p=128)
            osrc = poolF[:, O_XR:O_XR + 10240] \
                .rearrange("p (fc t) -> p fc t", t=640)[:, :, 128:640]
            S.wait("sync", m_fin[7])
            S.dma("sync", odst[:, 0:8], osrc[:, 0:8], "d_st")
            S.wait("sync", m_fin[15])
            m_out_store = S.dma("sync", odst[:, 8:16], osrc[:, 8:16], "d_st")

        S.wait("sync", S.mark("d_st"))
        S.wait("sync", S.mark("d_tap"))

        with nc.Block() as blk:
            @blk.sync
            def _(eng):
                for fn in S.q["sync"]:
                    fn(eng)

            @blk.gpsimd
            def _(eng):
                for fn in S.q["gpsimd"]:
                    fn(eng)

            @blk.vector
            def _(eng):
                for fn in S.q["vector"]:
                    fn(eng)

            @blk.scalar
            def _(eng):
                for fn in S.q["scalar"]:
                    fn(eng)

            @blk.tensor
            def _(eng):
                for fn in S.q["tensor"]:
                    fn(eng)

    return nc


_BUILD_CACHE = {}


def _host_inputs(inputs):
    x = np.asarray(inputs["L0_attn_x"], dtype=np.float32)
    w_attn = np.asarray(inputs["L0_attn_w_attn"], dtype=np.float32)
    w_o = np.asarray(inputs["L0_attn_w_o"], dtype=np.float32)
    amp_w = np.asarray(inputs["L0_ffn_amp_w"], dtype=np.float32)
    amp_b = np.asarray(inputs["L0_ffn_amp_b"], dtype=np.float32)
    ph_w = np.asarray(inputs["L0_ffn_ph_w"], dtype=np.float32)
    ph_b = np.asarray(inputs["L0_ffn_ph_b"], dtype=np.float32)
    ent_w = np.asarray(inputs["L0_ffn_ent_w"], dtype=np.float32)
    ent_b = np.asarray(inputs["L0_ffn_ent_b"], dtype=np.float32)
    ent_pw = np.asarray(inputs["L0_ffn_ent_pw"], dtype=np.float32)
    ent_pb = np.asarray(inputs["L0_ffn_ent_pb"], dtype=np.float32)
    path_w = np.asarray(inputs["L0_ffn_path_w"], dtype=np.float32)
    path_b = np.asarray(inputs["L0_ffn_path_b"], dtype=np.float32)
    phase = np.asarray(inputs["L0_ffn_phase"], dtype=np.float32)
    ln3_g = np.asarray(inputs["L0_ffn_ln3_g"], dtype=np.float32)
    ln3_b = np.asarray(inputs["L0_ffn_ln3_b"], dtype=np.float32)
    ln4_g = np.asarray(inputs["L0_ffn_ln4_g"], dtype=np.float32)
    ln4_b = np.asarray(inputs["L0_ffn_ln4_b"], dtype=np.float32)

    toks = np.ascontiguousarray(x.reshape(2 * 8192, D))

    def colize(v, n):
        return np.ascontiguousarray(v.reshape(n, 128).T)

    shared = {
        "w_qkvT": np.ascontiguousarray(w_attn.T),
        "w_oT": np.ascontiguousarray(w_o.T),
        "w_amp": np.ascontiguousarray(amp_w),
        "w_ph": np.ascontiguousarray(ph_w),
        "entpb_col": np.ascontiguousarray(
            np.stack([colize(ent_pb[l], 16) for l in range(2)])),
        "pathb_col": np.ascontiguousarray(
            np.stack([[colize(path_b[l, p], 16) for p in range(2)] for l in range(2)])),
        "phase_col": np.ascontiguousarray(
            np.stack([[colize(phase[l, p], 16) for p in range(2)] for l in range(2)])),
        "ampb_col": colize(amp_b, 16),
        "phb_col": colize(ph_b, 16),
        "ln3_gcol": np.ascontiguousarray(np.stack([colize(ln3_g[l], 16)
                                                   for l in range(2)])),
        "ln4_gcol": np.ascontiguousarray(np.stack([colize(ln4_g[l], 16)
                                                   for l in range(2)])),
        "ln3_bcol": np.ascontiguousarray(np.stack([colize(ln3_b[l], 16)
                                                   for l in range(2)])),
        "ln4_bcol": np.ascontiguousarray(np.stack([colize(ln4_b[l], 16)
                                                   for l in range(2)])),
    }
    sidx = np.arange(128)
    m_prev = (sidx[:, None] >= sidx[None, :]).astype(np.float32)
    m_cur = (sidx[:, None] <= sidx[None, :]).astype(np.float32)
    entwT = np.ascontiguousarray(np.transpose(ent_w, (0, 2, 1)))

    in_maps = []
    for c in range(NCORES):
        r0 = c * TPC
        xs = np.zeros((HALO + TPC, D), dtype=np.float32)
        if c in (0, 4):
            xs[HALO:] = toks[r0:r0 + TPC]
            m0 = np.zeros((128, 128), dtype=np.float32)
        else:
            xs[:] = toks[r0 - HALO:r0 + TPC]
            m0 = m_prev
        j0 = c * JSL
        d0 = c * 256
        im = dict(shared)
        im["x_fm"] = np.ascontiguousarray(xs.T)
        im["entwT_sl"] = np.ascontiguousarray(entwT[:, j0:j0 + JSL, :])
        im["entpw_sl"] = np.ascontiguousarray(ent_pw[:, j0:j0 + JSL, :])
        im["entb_col"] = np.ascontiguousarray(
            np.stack([colize(ent_b[l, j0:j0 + JSL], 8) for l in range(2)]))
        im["pathw_sl"] = np.ascontiguousarray(path_w[:, :, d0:d0 + 256, :])
        im["masks"] = np.ascontiguousarray(np.stack([m0, m_prev, m_cur]))
        in_maps.append(im)
    return in_maps


def kernel(**inputs):
    in_maps = _host_inputs(inputs)
    key = "main"
    if key not in _BUILD_CACHE:
        _BUILD_CACHE[key] = build()
    nc = _BUILD_CACHE[key]
    res = run_bass_kernel_spmd(nc, in_maps, core_ids=list(range(NCORES)))
    out = np.empty((2 * 8192, D), dtype=np.float32)
    for c in range(NCORES):
        out[c * TPC:(c + 1) * TPC] = np.asarray(res.results[c]["out_fm"]).T
    return out.reshape(2, 8192, D)


# version 13
# speedup vs baseline: 1.4173x; 1.0131x over previous
import numpy as np
import concourse.bass as bass
import concourse.mybir as mybir
from concourse.bass_utils import run_bass_kernel_spmd
from contextlib import ExitStack

F32 = mybir.dt.float32
BF16 = mybir.dt.bfloat16
AF = mybir.ActivationFunctionType
ALU = mybir.AluOpType
PI = float(np.pi)

NCORES = 8
D = 2048
KC = 16
TPC = 2048
HALO = 128
CH = 512
NCH = TPC // CH
JSL = 1024
FOLD_LROWS = 2049
INVN = 1.0 / 2048.0

ENGINES = ("sync", "gpsimd", "vector", "scalar", "tensor")


class Sched:
    def __init__(self, sems):
        self.q = {e: [] for e in ENGINES}
        self.sems = sems
        self.cnt = {name: 0 for name in sems}
        self.hw = {e: {name: 0 for name in sems} for e in ENGINES}

    ENG_SEM = {"vector": "s_dve", "scalar": "s_act", "gpsimd": "s_pool"}

    def op(self, eng, fn, sem=None, amt=1, is_dma=False):
        es = None if is_dma else self.ENG_SEM.get(eng)
        if es is not None and sem != "s_cc":
            self.wait(eng, (es, self.cnt[es]))
            self.cnt[es] += 1
            h = self.sems[es]
            self.q[eng].append(lambda e, fn=fn, h=h: fn(e).then_inc(h, 1))
            return (es, self.cnt[es])
        if sem is None:
            self.q[eng].append(fn)
            return None
        if es is not None and sem == "s_cc":
            self.wait(eng, (es, self.cnt[es]))
        self.cnt[sem] += amt
        h = self.sems[sem]
        self.q[eng].append(lambda e, fn=fn, h=h, a=amt: fn(e).then_inc(h, a))
        return (sem, self.cnt[sem])

    def mark(self, sem):
        return (sem, self.cnt[sem])

    def wait(self, eng, ms):
        if ms is None:
            return
        sem, val = ms
        if self.hw[eng][sem] >= val:
            return
        self.hw[eng][sem] = val
        h = self.sems[sem]
        self.q[eng].append(lambda e, h=h, v=val: e.wait_ge(h, v))

    def waits(self, eng, mss):
        best = {}
        for ms in mss:
            if ms is None:
                continue
            sem, val = ms
            if best.get(sem, -1) < val:
                best[sem] = val
        for sem, val in best.items():
            self.wait(eng, (sem, val))

    def dma(self, eng, out_ap, in_ap, sem, **kw):
        return self.op(eng, lambda e: e.dma_start(out_ap, in_ap, **kw), sem, 16,
                       is_dma=True)


def build(taps=()):
    taps = set(taps)
    nc = bass.Bass("TRN2", target_bir_lowering=False)

    def inp(name, shape, dtype=F32):
        return nc.dram_tensor(name, shape, dtype, kind="ExternalInput")

    x_fm = inp("x_fm", [D, HALO + TPC])
    w_qkvT = inp("w_qkvT", [D, 3 * D])
    w_oT = inp("w_oT", [D, D])
    w_amp = inp("w_amp", [D, D])
    w_ph = inp("w_ph", [D, D])
    entwT_sl = inp("entwT_sl", [2, JSL, D])
    entpw_sl = inp("entpw_sl", [2, JSL, D])
    entb_col = inp("entb_col", [2, 128, JSL // 128])
    entpb_col = inp("entpb_col", [2, 128, 16])
    pathw_sl = inp("pathw_sl", [2, 2, 256, D])
    pathb_col = inp("pathb_col", [2, 2, 128, 16])
    phase_col = inp("phase_col", [2, 2, 128, 16])
    ampb_col = inp("ampb_col", [128, 16])
    phb_col = inp("phb_col", [128, 16])
    ln3_gcol = inp("ln3_gcol", [2, 128, 16])
    ln4_gcol = inp("ln4_gcol", [2, 128, 16])
    ln3_bcol = inp("ln3_bcol", [2, 128, 16])
    ln4_bcol = inp("ln4_bcol", [2, 128, 16])
    masks_in = inp("masks", [3, 128, 128])

    out_fm = nc.dram_tensor("out_fm", [D, TPC], F32, kind="ExternalOutput")
    wbf = nc.dram_tensor("wbf", [D, 6 * D], BF16, kind="Internal")
    r1s = nc.dram_tensor("r1s", [D, TPC], F32, kind="Internal")
    fold_in = nc.dram_tensor("fold_in", [2 * FOLD_LROWS, D], BF16, kind="Internal")
    fold_out = nc.dram_tensor("fold_out", [2 * FOLD_LROWS, D], BF16, kind="Internal")
    modrow_d = nc.dram_tensor("modrow_d", [4, D], F32, kind="Internal")
    pg_in = nc.dram_tensor("pg_in", [512, D], BF16, kind="Internal")
    pg_out = nc.dram_tensor("pg_out", [NCORES * 512, D], BF16, kind="Internal")

    tap_t = {}
    tapdefs = {
        "lnx0": ([128, 10240], BF16), "r1": ([D, TPC], F32),
        "mfold": ([2 * FOLD_LROWS, D], BF16), "pg": ([NCORES * 512, D], BF16),
        "h0": ([128, 8192], F32), "pre3": ([128, 8192], F32),
        "h1": ([128, 8192], F32), "araw0": ([128, 8192], BF16),
        "qbf0": ([128, 8192], BF16), "kbf0": ([128, 10240], BF16),
        "vbf0": ([128, 10240], BF16), "lnr10": ([128, 10240], BF16),
        "modrow": ([4, D], F32), "pgin": ([512, D], BF16), "colm": ([128, 64], F32),
        "pre4": ([128, 8192], F32), "h2": ([128, 8192], F32),
        "pre3b": ([128, 8192], F32), "h4": ([128, 8192], F32),
        "qbfa": ([128, 8192], BF16), "qbfb": ([128, 8192], BF16),
        "ptld": ([128, 32], F32), "btot": ([128, 32], F32),
    }
    for t in taps:
        shape, dt = tapdefs[t]
        tap_t[t] = nc.dram_tensor("tap_" + t, shape, dt, kind="ExternalOutput")

    with ExitStack() as st:
        NP = 49920
        pool = st.enter_context(nc.sbuf_tensor("pool", [128, NP], F32))
        poolF = pool.ap()
        poolH = pool.ap().bitcast(BF16)

        def fview(off, width, p0=0, pn=128):
            return poolF[p0:p0 + pn, off:off + width]

        def hview(off_u, width_bf, p0=0, pn=128):
            return poolH[p0:p0 + pn, 2 * off_u: 2 * off_u + width_bf]

        O_XR, O_LNBF, O_QBF, O_SLAB = 0, 10240, 15360, 19456
        O_TS, O_ROWS, O_ATT, O_U = 27648, 30720, 32768, 33536

        XR = [fview(O_XR + i * 640, 640) for i in range(KC)]
        LNBF = [hview(O_LNBF + i * 320, 640) for i in range(KC)]
        QBF = [hview(O_QBF + i * 256, 512) for i in range(KC)]
        TS = [fview(O_TS + i * 512, 512) for i in range(6)]
        TS0BF = hview(O_TS, 512)
        S1 = fview(O_ATT, 256)
        S2 = fview(O_ATT + 256, 256)
        ATT = [hview(O_ATT + 512 + i * 128, 256) for i in range(2)]
        rows_a = fview(O_ROWS, 512, 0, 1)
        rows_b = fview(O_ROWS + 512, 512, 0, 1)
        rows_rstd = fview(O_ROWS + 1024, 512, 0, 1)
        rows2_bf = hview(O_ROWS + 1536, 512, 0, 2)
        onesf_row = fview(O_ROWS + 1792, 128, 0, 1)
        onesb_row = hview(O_ROWS + 1920, 128, 0, 1)
        U = O_U
        FW_T = [hview(U + j * 1024, 2048) for j in range(8)]
        PWB = [U + 8192, U + 8192 + 2048]
        FSTG = [hview(U + 12288 + j * 256, 512) for j in range(4)]
        BSTG = hview(U + 13312, 2048, 0, 1)
        PW0 = fview(U + 0, 2048)
        PW1 = fview(U + 2048, 2048)
        MODROW = [fview(U + 8192 + i * 2048, 2048, 0, 1) for i in range(2)]
        PSTG = [hview(U + 12288 + j * 1024, 2048) for j in range(2)]
        KBF = [hview(U + i * 320, 640) for i in range(KC)]
        VBF = [hview(U + 5120 + i * 1024, 2048) for i in range(5)]
        ARAW = [hview(U + 10240 + i * 256, 512) for i in range(KC)]
        HA = [fview(U + i * 512, 512) for i in range(KC)]
        HB = [fview(U + 8192 + i * 512, 512) for i in range(KC)]

        def smallt(name, shape, dtype=F32):
            return st.enter_context(nc.sbuf_tensor(name, shape, dtype))

        masks = smallt("masks_sb", [128, 3 * 128])
        colp = smallt("colp", [128, 64])
        colm = smallt("colm", [128, 64])
        colw = smallt("colw", [128, 64])
        col_ampb = smallt("col_ampb", [128, 16])
        col_phb = smallt("col_phb", [128, 16])
        col_entb = smallt("col_entb", [128, 16], BF16)
        col_entpb = smallt("col_entpb", [128, 32])
        col_btot = smallt("col_btot", [128, 32])
        col_ptld = smallt("col_ptld", [128, 32])
        col_pb = smallt("col_pb", [128, 64])
        col_g3 = smallt("col_g3", [128, 32])
        col_g4 = smallt("col_g4", [128, 32])
        col_b3 = smallt("col_b3", [128, 32])
        col_b4 = smallt("col_b4", [128, 32])
        ones_col_bf = smallt("ones_col_bf", [128, 1], BF16)
        pihalf_col = smallt("pihalf_col", [128, 1])
        col_m2 = smallt("col_m2", [128, 1])
        col_h05 = smallt("col_h05", [128, 1])
        eps1 = smallt("eps1", [1, 1])

        def mview(i):
            return masks.ap()[:, i * 128:(i + 1) * 128]

        PM = [st.enter_context(nc.psum_tensor(f"pm{i}", [128, 512], F32)) for i in range(4)]
        PA = [st.enter_context(nc.psum_tensor(f"pa{i}", [128, 512], F32)) for i in range(2)]
        PS = st.enter_context(nc.psum_tensor("ps", [128, 512], F32))
        PF = st.enter_context(nc.psum_tensor("pf", [128, 512], F32))
        PMPA = PM + PA

        semnames = ["s_pe", "s_act", "s_dve", "s_pool", "s_cc",
                    "d_sm", "d_sme", "d_mw", "d_fw", "d_pw0", "d_pw1",
                    "d_fst0", "d_fst1", "d_fst2", "d_fst3", "d_bst", "d_pf",
                    "d_x", "d_st", "d_tap", "d_sl0", "d_sl1", "d_sl0s", "d_sl1s",
                    "d_gb", "d_bt", "d_wc0", "d_wc1", "d_wc2", "d_wc3"]
        sems = {n: st.enter_context(nc.semaphore(n)) for n in semnames}
        S = Sched(sems)

        bank_free = {}
        slab_cons = {0: None, 1: None}
        ln_guard = {"p1": {0: None, 1: None}, "sq": None,
                    "ts1": [None, None], "ts2": [None, None]}

        def mm(ps_ap, lhsT_ap, rhs_ap, start, stop, sem=None):
            return S.op("tensor",
                        lambda e, o=ps_ap, l=lhsT_ap, r=rhs_ap, a=start, b=stop:
                        e.matmul(o, l, r, start=a, stop=b), sem=sem)

        def slab_load(slot, src2d, queue=None, n_kc=16, fwidth=512):
            queue = "sync"
            S.wait(queue, slab_cons[slot])
            src3 = src2d.rearrange("(kc p) f -> p kc f", p=128)
            dst3 = poolH[:, 2 * (O_SLAB + slot * 4096):
                         2 * (O_SLAB + slot * 4096) + n_kc * fwidth] \
                .rearrange("p (kc f) -> p kc f", f=fwidth)
            sem = f"d_sl{slot}s"
            h = n_kc // 2
            S.dma(queue, dst3[:, 0:h], src3[:, 0:h], sem)
            return S.dma(queue, dst3[:, h:n_kc], src3[:, h:n_kc], sem)

        def slab_tile(slot, kc, f0, fw, fwidth=512):
            base = 2 * (O_SLAB + slot * 4096) + kc * fwidth + f0
            return poolH[:, base:base + fw]

        def tap_sbuf(name, region_ap, mss):
            if name not in tap_t:
                return None
            S.waits("sync", mss)
            return S.dma("sync", tap_t[name].ap(), region_ap, "d_tap")

        def tap_dram(name, src_ap, mss):
            if name in tap_t:
                S.waits("sync", mss)
                S.dma("sync", tap_t[name].ap(), src_ap, "d_tap")

        def emit_ln(X, bf_tgt, fp_tgt, ttiles, gcol, bcol, pre_ms=(), pre_fc=None):
            last_ms = None
            fc_ms = [None] * KC
            for (c0, w) in ttiles:
                cast_fc = [None] * KC
                for fc in range(KC):
                    eng = "scalar" if fc % 2 == 0 else "vector"
                    S.waits(eng, pre_ms)
                    if pre_fc is not None:
                        S.wait(eng, pre_fc[fc])
                    dsta, srca = bf_tgt[fc][:, c0:c0 + w], X[fc][:, c0:c0 + w]
                    if eng == "scalar":
                        cast_fc[fc] = S.op(
                            eng, lambda e, d=dsta, s=srca:
                            e.activation(d, s, AF.Copy), sem="s_act")
                    else:
                        cast_fc[fc] = S.op(
                            eng, lambda e, d=dsta, s=srca: e.tensor_copy(d, s))
                S.wait("tensor", bank_free.get("ps"))
                S.wait("tensor", bank_free.get("pf"))
                S.wait("tensor", m_sets)
                S.wait("scalar", m_sets)
                prev_sq = ln_guard["sq"]
                for fc in range(KC):
                    S.waits("scalar", pre_ms)
                    if pre_fc is not None:
                        S.wait("scalar", pre_fc[fc])
                    S.wait("scalar", prev_sq)
                    dsta, srca = TS0BF[:, 0:w], X[fc][:, c0:c0 + w]
                    sq_ms = S.op("scalar",
                                 lambda e, d=dsta, s=srca: e.activation(d, s, AF.Square),
                                 sem="s_act")
                    S.wait("tensor", cast_fc[fc])
                    S.wait("tensor", sq_ms)
                    mm(PS.ap()[0:1, 0:w], ones_col_bf.ap(), bf_tgt[fc][:, c0:c0 + w],
                       start=(fc == 0), stop=(fc == KC - 1))
                    prev_sq = mm(PF.ap()[0:1, 0:w], ones_col_bf.ap(), TS0BF[:, 0:w],
                                 start=(fc == 0), stop=(fc == KC - 1),
                                 sem="s_pe")
                stats_pe = prev_sq
                ln_guard["sq"] = prev_sq
                S.wait("scalar", stats_pe)
                S.op("scalar", lambda e, w=w: e.activation(
                    rows_a[:, 0:w], PS.ap()[0:1, 0:w], AF.Square, scale=INVN))
                m_t1 = S.op("scalar", lambda e, w=w: e.activation(
                    rows_b[:, 0:w], PF.ap()[0:1, 0:w], AF.Copy, scale=INVN),
                    sem="s_act")
                S.wait("vector", m_t1)
                m_ve = S.op("vector", lambda e, w=w: e.tensor_tensor(
                    rows_b[:, 0:w], rows_b[:, 0:w], rows_a[:, 0:w], ALU.subtract),
                    sem="s_dve")
                S.wait("scalar", m_ve)
                S.op("scalar", lambda e, w=w: e.activation(
                    rows_a[:, 0:w], rows_b[:, 0:w], AF.Ln, bias=eps1.ap()))
                m_rstd = S.op("scalar", lambda e, w=w: e.activation(
                    rows_rstd[:, 0:w], rows_a[:, 0:w], AF.Exp, scale=-0.5), sem="s_act")
                m_mrow = S.op("scalar", lambda e, w=w: e.activation(
                    rows_a[:, 0:w], PS.ap()[0:1, 0:w], AF.Copy, scale=INVN),
                    sem="s_act")
                S.wait("tensor", m_rstd)
                m_bc = mm(PF.ap()[:, 0:w], onesf_row, rows_rstd[:, 0:w],
                          start=True, stop=True, sem="s_pe")
                S.wait("tensor", m_mrow)
                m_bc2 = mm(PS.ap()[:, 0:w], onesf_row, rows_a[:, 0:w],
                           start=True, stop=True, sem="s_pe")
                S.wait("scalar", m_bc)
                S.waits("scalar", ln_guard["ts1"])
                m_bce = S.op("scalar", lambda e, w=w: e.activation(
                    TS[1][:, 0:w], PF.ap()[:, 0:w], AF.Copy), sem="s_act")
                bank_free["pf"] = m_bce
                S.wait("scalar", m_bc2)
                S.waits("scalar", ln_guard["ts2"])
                m_bce2 = S.op("scalar", lambda e, w=w: e.activation(
                    TS[2][:, 0:w], PS.ap()[:, 0:w], AF.Copy), sem="s_act")
                bank_free["ps"] = m_bce2
                affine = gcol is not None
                for fc in range(KC):
                    par = fc % 2
                    eng1 = "gpsimd" if par == 0 else "vector"
                    S.wait(eng1, m_bce2)
                    S.wait(eng1, ln_guard["p1"][par])
                    t1d, t1s = TS[3 + par][:, 0:w], X[fc][:, c0:c0 + w]
                    m_p1 = S.op(eng1, lambda e, d=t1d, s=t1s, w=w:
                                e.tensor_tensor(d, s, TS[2][:, 0:w], ALU.subtract),
                                sem="s_pool")
                    ln_guard["ts2"][par] = m_p1
                    tgt = fp_tgt[fc][:, 0:w] if fp_tgt is not None \
                        else bf_tgt[fc][:, c0:c0 + w]
                    S.wait(eng1, m_bce)
                    if affine:
                        m_t2 = S.op(eng1, lambda e, d=t1d, w=w:
                                    e.tensor_tensor(d, d, TS[1][:, 0:w], ALU.mult),
                                    sem="s_pool")
                        ln_guard["ts1"][par] = m_t2
                        gsc = gcol[:, fc:fc + 1]
                        bsc = bcol[:, fc:fc + 1]
                        S.wait("vector", m_t2)
                        m_out = S.op("vector", lambda e, t=tgt, d=t1d, g=gsc, b=bsc:
                                     e.tensor_scalar(t, d, g, b, ALU.mult, ALU.add),
                                     sem="s_dve")
                    else:
                        m_out = S.op(eng1, lambda e, t=tgt, d=t1d, w=w:
                                     e.tensor_tensor(t, d, TS[1][:, 0:w], ALU.mult),
                                     sem="s_pool")
                        ln_guard["ts1"][par] = m_out
                    ln_guard["p1"][par] = m_out
                    last_ms = m_out
                    if fp_tgt is not None and bf_tgt is not None:
                        S.wait("scalar", m_out)
                        cd, cs2 = bf_tgt[fc][:, c0:c0 + w], fp_tgt[fc][:, 0:w]
                        last_ms = S.op("scalar", lambda e, d=cd, s=cs2:
                                       e.activation(d, s, AF.Copy), sem="s_act")
                    fc_ms[fc] = last_ms
            return last_ms, fc_ms

        S.dma("sync", masks.ap().rearrange("p (m f) -> p m f", f=128),
              masks_in.ap().rearrange("m p f -> p m f"), "d_sm")
        S.dma("sync", col_ampb.ap(), ampb_col.ap(), "d_sm")
        S.dma("sync", col_phb.ap(), phb_col.ap(), "d_sm")
        S.dma("sync", col_entpb.ap().rearrange("p (l f) -> p l f", f=16),
              entpb_col.ap().rearrange("l p f -> p l f"), "d_sm")
        S.dma("sync", col_pb.ap().rearrange("p (l q f) -> p l q f", q=2, f=16),
              pathb_col.ap().rearrange("l q p f -> p l q f"), "d_sm")
        S.dma("sync", colp.ap().rearrange("p (l q f) -> p l q f", q=2, f=16),
              phase_col.ap().rearrange("l q p f -> p l q f"), "d_sm")
        S.dma("sync", col_g3.ap().rearrange("p (l f) -> p l f", f=16),
              ln3_gcol.ap().rearrange("l p f -> p l f"), "d_sm")
        S.dma("sync", col_g4.ap().rearrange("p (l f) -> p l f", f=16),
              ln4_gcol.ap().rearrange("l p f -> p l f"), "d_sm")
        S.dma("sync", col_b3.ap().rearrange("p (l f) -> p l f", f=16),
              ln3_bcol.ap().rearrange("l p f -> p l f"), "d_sm")
        m_small = S.dma("sync", col_b4.ap().rearrange("p (l f) -> p l f", f=16),
                        ln4_bcol.ap().rearrange("l p f -> p l f"), "d_sm")
        m_entb = S.dma("gpsimd", col_entb.ap().rearrange("p (l f) -> p l f", f=8),
                       entb_col.ap().rearrange("l p f -> p l f"), "d_sme")

        m_wcp = [None] * 24

        def emit_wcp(pjs):
            for pj in pjs:
                if pj < 12:
                    src = w_qkvT.ap()[:, pj * 512:(pj + 1) * 512]
                elif pj < 16:
                    src = w_oT.ap()[:, (pj - 12) * 512:(pj - 11) * 512]
                elif pj < 20:
                    src = w_amp.ap()[:, (pj - 16) * 512:(pj - 15) * 512]
                else:
                    src = w_ph.ap()[:, (pj - 20) * 512:(pj - 19) * 512]
                m_wcp[pj] = S.dma("gpsimd", wbf.ap()[:, pj * 512:(pj + 1) * 512],
                                  src, "d_wc0")
        S.op("vector", lambda e: e.memset(ones_col_bf.ap(), 1.0))
        S.op("vector", lambda e: e.memset(pihalf_col.ap(), PI / 2))
        S.op("vector", lambda e: e.memset(col_m2.ap(), -2.0))
        S.op("vector", lambda e: e.memset(col_h05.ap(), 0.5))
        S.op("vector", lambda e: e.memset(eps1.ap(), 1e-5))
        S.op("vector", lambda e: e.memset(onesf_row, 1.0))
        S.op("vector", lambda e: e.memset(onesb_row, 1.0))
        m_sets = S.op("vector", lambda e: e.memset(rows2_bf, 1.0), sem="s_dve")

        S.wait("vector", m_small)
        m_ampbh = S.op("vector", lambda e: e.tensor_scalar_mul(
            col_ampb.ap(), col_ampb.ap(), 0.5), sem="s_dve")

        S.wait("gpsimd", m_small)
        S.op("gpsimd", lambda e: e.tensor_scalar(colw.ap(), colp.ap(), PI, None, ALU.is_gt))
        S.op("gpsimd", lambda e: e.tensor_scalar_mul(colw.ap(), colw.ap(), -2 * PI))
        S.op("gpsimd", lambda e: e.tensor_tensor(colw.ap(), colw.ap(), colp.ap(), ALU.add))
        S.op("gpsimd", lambda e: e.tensor_scalar(colm.ap(), colw.ap(), -PI, None, ALU.is_lt))
        S.op("gpsimd", lambda e: e.tensor_scalar_mul(colm.ap(), colm.ap(), 2 * PI))
        m_wrap = S.op("gpsimd", lambda e: e.tensor_tensor(
            colw.ap(), colm.ap(), colw.ap(), ALU.add), sem="s_pool")
        S.wait("scalar", m_wrap)
        S.wait("scalar", m_sets)
        m_sin = S.op("scalar", lambda e: e.activation(colm.ap(), colw.ap(), AF.Sin),
                     sem="s_act")
        S.wait("gpsimd", m_sin)
        S.op("gpsimd", lambda e: e.tensor_scalar(colp.ap(), colw.ap(), PI / 2, None,
                                                 ALU.is_gt))
        S.op("gpsimd", lambda e: e.tensor_scalar_mul(colp.ap(), colp.ap(), -2 * PI))
        m_wr2 = S.op("gpsimd", lambda e: e.tensor_tensor(
            colw.ap(), colp.ap(), colw.ap(), ALU.add), sem="s_pool")
        S.wait("scalar", m_wr2)
        m_cos = S.op("scalar", lambda e: e.activation(colp.ap(), colw.ap(), AF.Sin,
                                                      bias=pihalf_col.ap()), sem="s_act")
        S.wait("vector", m_cos)
        m_mod = S.op("vector", lambda e: e.tensor_tensor(colm.ap(), colm.ap(), colp.ap(),
                                                         ALU.add), sem="s_dve")
        S.wait("gpsimd", m_mod)
        for l in range(2):
            S.op("gpsimd", lambda e, l=l: e.tensor_tensor(
                colw.ap()[:, 0:16], col_pb.ap()[:, (2 * l) * 16:(2 * l + 1) * 16],
                colm.ap()[:, (2 * l) * 16:(2 * l + 1) * 16], ALU.mult))
            S.op("gpsimd", lambda e, l=l: e.tensor_tensor(
                colw.ap()[:, 16:32], col_pb.ap()[:, (2 * l + 1) * 16:(2 * l + 2) * 16],
                colm.ap()[:, (2 * l + 1) * 16:(2 * l + 2) * 16], ALU.mult))
            S.op("gpsimd", lambda e, l=l: e.tensor_tensor(
                col_ptld.ap()[:, l * 16:(l + 1) * 16], colw.ap()[:, 0:16],
                colw.ap()[:, 16:32], ALU.add))
        m_mr_st = []

        def emit_modrows():
            for l in range(2):
                for p in range(2):
                    S.wait("gpsimd", m_mod)
                    m_mr_st.append(S.dma(
                        "gpsimd",
                        modrow_d.ap()[2 * l + p, :].rearrange("(a p) -> p a", p=128),
                        colm.ap()[:, (2 * l + p) * 16:(2 * l + p) * 16 + 16],
                        "d_mw", allow_slow_non_contiguous=True))

        pw_cons = [None, None]
        stg_free = [None] * 4
        bstg_free = None
        fold_mm_last = None
        fold_store_last = None
        for l in range(2):
            wt_ms = []
            S.wait("gpsimd", fold_mm_last)
            for j in range(8):
                wt_ms.append(S.dma("gpsimd", FW_T[j][:, :],
                                   entwT_sl.ap()[l, j * 128:(j + 1) * 128, :], "d_fw"))
            m_bev = None
            for fs in range(4):
                slot = fs % 2
                S.wait("gpsimd", pw_cons[slot])
                pwdst = poolH[:, 2 * PWB[slot]: 2 * PWB[slot] + 8 * 512] \
                    .rearrange("p (jc f) -> p jc f", f=512)
                pwsrc = entpw_sl.ap()[l, :, fs * 512:(fs + 1) * 512] \
                    .rearrange("(jc p) f -> p jc f", p=128)
                m_pw = S.dma("gpsimd", pwdst, pwsrc, f"d_pw{slot}")
                emit_wcp([2 * fs, 2 * fs + 1] if l == 0 else [8 + fs])
                S.waits("tensor", wt_ms)
                S.wait("tensor", m_pw)
                S.wait("tensor", m_entb)

                def pwtile(j, slot=slot):
                    base = 2 * PWB[slot] + j * 512
                    return poolH[:, base:base + 512]

                S.wait("tensor", bank_free.get("pf"))
                m_b = None
                for j in range(8):
                    m_b = mm(PF.ap()[0:1, 0:512],
                             col_entb.ap()[:, l * 8 + j:l * 8 + j + 1], pwtile(j),
                             start=(j == 0), stop=(j == 7),
                             sem="s_pe" if j == 7 else None)
                S.wait("scalar", m_b)
                S.wait("scalar", bstg_free)
                m_bev = S.op("scalar", lambda e, fs=fs: e.activation(
                    BSTG[:, fs * 512:(fs + 1) * 512], PF.ap()[0:1, 0:512], AF.Copy),
                    sem="s_act")
                bank_free["pf"] = m_bev
                for dc in range(KC):
                    bk = PM[dc % 2]
                    S.wait("tensor", bank_free.get(bk.name))
                    m_mmv = None
                    for j in range(8):
                        m_mmv = mm(bk.ap()[:, 0:512],
                                   FW_T[j][:, dc * 128:(dc + 1) * 128], pwtile(j),
                                   start=(j == 0), stop=(j == 7),
                                   sem="s_pe" if j == 7 else None)
                    fold_mm_last = m_mmv
                    sg = dc % 4
                    S.wait("scalar", stg_free[sg])
                    S.wait("scalar", m_mmv)
                    m_ev = S.op("scalar", lambda e, sg=sg, dc=dc: e.activation(
                        FSTG[sg][:, 0:512], PM[dc % 2].ap()[:, 0:512], AF.Copy),
                        sem="s_act")
                    bank_free[bk.name] = m_ev
                    S.wait("sync", m_ev)
                    m_stx = S.dma("sync",
                                  fold_in.ap()[l * FOLD_LROWS + dc * 128:
                                               l * FOLD_LROWS + (dc + 1) * 128,
                                               fs * 512:(fs + 1) * 512],
                                  FSTG[sg][:, 0:512], f"d_fst{sg}")
                    stg_free[sg] = m_stx
                    fold_store_last = m_stx
                pw_cons[slot] = fold_mm_last
            S.wait("sync", m_bev)
            S.wait("sync", bstg_free)
            fold_store_last = S.dma(
                "sync",
                fold_in.ap()[l * FOLD_LROWS + 2048: l * FOLD_LROWS + 2049, :],
                BSTG, "d_bst")
            bstg_free = fold_store_last

        emit_modrows()
        emit_wcp(range(12, 24))
        fst_marks = [S.mark("d_fst0"), S.mark("d_fst1"), S.mark("d_fst2"),
                     S.mark("d_fst3"), S.mark("d_bst")]

        def emit_allreduce():
            S.waits("gpsimd", fst_marks)
            S.op("gpsimd", lambda e: e.collective_compute(
                "AllReduce", ALU.add, replica_groups=[list(range(NCORES))],
                ins=[fold_in.ap().opt()], outs=[fold_out.ap().opt()]), sem="s_cc")

        pf_state = {}

        def emit_pfold():
            pfold_dve_last = None
            pstg_store = [None, None]
            for l in range(2):
                mr_ld = []
                for p in range(2):
                    S.wait("sync", fold_mm_last)
                    S.wait("sync", S.mark("s_pe"))
                    S.waits("sync", m_mr_st)
                    mr_ld.append(S.dma("sync", MODROW[p], modrow_d.ap()[2 * l + p:2 * l + p + 1, :],
                                       "d_pf"))
                for rt in range(2):
                    S.wait("sync", fold_mm_last)
                    S.wait("sync", pfold_dve_last)
                    m_pw0 = S.dma("sync", PW0, pathw_sl.ap()[l, 0, rt * 128:(rt + 1) * 128, :],
                                  "d_pf")
                    m_pw1 = S.dma("sync", PW1, pathw_sl.ap()[l, 1, rt * 128:(rt + 1) * 128, :],
                                  "d_pf")
                    m_pf_all = S.mark("d_pf")
                    for fs in range(4):
                        S.wait("tensor", bank_free.get("pf"))
                        S.wait("tensor", m_pf_all)
                        m_k1 = mm(PF.ap()[:, 0:512], onesf_row,
                                  MODROW[0][:, fs * 512:(fs + 1) * 512],
                                  start=True, stop=True, sem="s_pe")
                        S.wait("vector", m_k1)
                        S.wait("vector", m_pf_all)
                        m_t1 = S.op("vector", lambda e, fs=fs: e.tensor_tensor(
                            TS[4][:, 0:512], PW0[:, fs * 512:(fs + 1) * 512],
                            PF.ap()[:, 0:512], ALU.mult), sem="s_dve")
                        bank_free["pf"] = m_t1
                        S.wait("tensor", m_t1)
                        m_k2 = mm(PF.ap()[:, 0:512], onesf_row,
                                  MODROW[1][:, fs * 512:(fs + 1) * 512],
                                  start=True, stop=True, sem="s_pe")
                        S.wait("vector", m_k2)
                        m_t2 = S.op("vector", lambda e, fs=fs: e.tensor_tensor(
                            TS[5][:, 0:512], PW1[:, fs * 512:(fs + 1) * 512],
                            PF.ap()[:, 0:512], ALU.mult), sem="s_dve")
                        bank_free["pf"] = m_t2
                        S.waits("vector", stg_free + [bstg_free, pstg_store[rt]])
                        pfold_dve_last = S.op("vector", lambda e, rt=rt, fs=fs: e.tensor_tensor(
                            PSTG[rt][:, fs * 512:(fs + 1) * 512], TS[4][:, 0:512],
                            TS[5][:, 0:512], ALU.add), sem="s_dve")
                    S.wait("sync", pfold_dve_last)
                    pstg_store[rt] = S.dma(
                        "sync", pg_in.ap()[l * 256 + rt * 128: l * 256 + (rt + 1) * 128, :],
                        PSTG[rt], f"d_fst{rt}")
            S.wait("gpsimd", pstg_store[0])
            S.wait("gpsimd", pstg_store[1])
            m_ag = S.op("gpsimd", lambda e: e.collective_compute(
                "AllGather", ALU.bypass, replica_groups=[list(range(NCORES))],
                ins=[pg_in.ap().opt()], outs=[pg_out.ap().opt()]), sem="s_cc")

            pf_state["dve"] = pfold_dve_last
            pf_state["st"] = [pstg_store[0], pstg_store[1]]

        tap_dram("mfold", fold_out.ap(), [S.mark("s_cc")])
        tap_dram("pg", pg_out.ap(), [S.mark("s_cc")])
        tap_dram("modrow", modrow_d.ap(), [S.mark("s_cc")])
        tap_dram("pgin", pg_in.ap(), [S.mark("s_cc")])
        tap_sbuf("colm", colm.ap(), [S.mark("s_cc")])

        tap_sbuf("ptld", col_ptld.ap(), [S.mark("s_dve")])
        m_Ufree = (fold_mm_last,)

        m_r1_store = [None] * NCH
        m_x_ld = None
        prev_attn_pe = None
        prev_qkv_pe = None
        attn_guard = {"relu3": None, "att3": None, "out2": None, "out3": None}
        m_aend = None

        for c in range(NCH):
            if c == 1:
                emit_allreduce()
            if c == 0:
                toff, wch, gc0 = 0, 640, 0
                ln_tt = [(0, 512), (512, 128)]
                kv_tt = [(0, 512), (512, 128)]
                v_tps = [0, 1, 2, 3, 4]
            else:
                toff, wch, gc0 = 128, 512, 128 + c * 512
                ln_tt = [(128, 512)]
                kv_tt = [(128, 512)]
                v_tps = [1, 2, 3, 4]

            S.wait("sync", m_r1_store[c - 1] if c > 0 else None)
            xdst = poolF[:, O_XR:O_XR + 10240] \
                .rearrange("p (fc t) -> p fc t", t=640)[:, :, toff:toff + wch]
            xsrc = x_fm.ap()[:, gc0:gc0 + wch].rearrange("(fc p) t -> p fc t", p=128)
            m_x_a = S.dma("sync", xdst[:, 0:8], xsrc[:, 0:8], "d_x")
            m_x_ld = S.dma("sync", xdst[:, 8:16], xsrc[:, 8:16], "d_x")

            m_tail = None
            if c > 0:
                S.wait("gpsimd", prev_attn_pe)
                for fc in range(KC):
                    S.op("gpsimd", lambda e, fc=fc: e.tensor_copy(
                        KBF[fc][:, 0:128], KBF[fc][:, 512:640]))
                m_tail = S.op("gpsimd", lambda e: e.tensor_copy(
                    VBF[0][:, :], VBF[4][:, :]), sem="s_pool")

            ln_pre = [prev_qkv_pe]
            x_fcms = [m_x_a] * 8 + [m_x_ld] * 8
            m_lnx, m_lnx_fc = emit_ln(XR, LNBF, None, ln_tt, None, None,
                                      pre_ms=ln_pre, pre_fc=x_fcms)
            if c == 0:
                tap_sbuf("lnx0", poolH[:, 2 * O_LNBF: 2 * O_LNBF + 10240], [m_lnx])

            m_qkv_ev = []
            for fg in range(12):
                slot = fg % 2
                S.wait("sync", m_wcp[fg])
                m_slab = slab_load(slot, wbf.ap()[:, fg * 512:(fg + 1) * 512])
                S.wait("tensor", m_slab)
                S.wait("tensor", m_lnx)
                if fg < 8:
                    is_q = fg < 4
                    for fcg in range(4):
                        fc = (fg % 4) * 4 + fcg
                        bk = PMPA[(fg * 4 + fcg) % 6]
                        tts = [(128, 512)] if is_q else kv_tt
                        for (t0, tw) in tts:
                            S.wait("tensor", bank_free.get(bk.name))
                            m_g = None
                            for kc in range(KC):
                                S.wait("tensor", m_lnx_fc[kc])
                                m_g = mm(bk.ap()[:, 0:tw],
                                         slab_tile(slot, kc, fcg * 128, 128),
                                         LNBF[kc][:, t0:t0 + tw],
                                         start=(kc == 0), stop=(kc == KC - 1),
                                         sem="s_pe" if kc == KC - 1 else None)
                            if is_q:
                                dst = QBF[fc][:, t0 - 128:t0 - 128 + tw]
                            else:
                                dst = KBF[fc][:, t0:t0 + tw]
                                S.waits("scalar", m_Ufree)
                                if m_tail:
                                    S.wait("scalar", m_tail)
                            S.wait("scalar", m_g)
                            m_e = S.op("scalar", lambda e, d=dst, b=bk, tw=tw:
                                       e.activation(d, b.ap()[:, 0:tw], AF.Copy),
                                       sem="s_act")
                            bank_free[bk.name] = m_e
                            m_qkv_ev.append(m_e)
                else:
                    vs = fg - 8
                    for tp in v_tps:
                        bk = PM[tp % 2]
                        S.wait("tensor", bank_free.get(bk.name))
                        m_g = None
                        for kc in range(KC):
                            S.wait("tensor", m_lnx_fc[kc])
                            m_g = mm(bk.ap()[:, 0:512],
                                     LNBF[kc][:, tp * 128:(tp + 1) * 128],
                                     slab_tile(slot, kc, 0, 512),
                                     start=(kc == 0), stop=(kc == KC - 1),
                                     sem="s_pe" if kc == KC - 1 else None)
                        S.waits("scalar", m_Ufree)
                        if m_tail:
                            S.wait("scalar", m_tail)
                        S.wait("scalar", m_g)
                        m_e = S.op("scalar", lambda e, tp=tp, vs=vs, b=bk: e.activation(
                            VBF[tp][:, vs * 512:(vs + 1) * 512], b.ap()[:, 0:512],
                            AF.Copy), sem="s_act")
                        bank_free[bk.name] = m_e
                        m_qkv_ev.append(m_e)
                slab_cons[slot] = S.mark("s_pe")
            prev_qkv_pe = S.mark("s_pe")
            if c == 0:
                tap_sbuf("qbf0", poolH[:, 2 * O_QBF: 2 * O_QBF + 8192], m_qkv_ev[-1:])
                tap_sbuf("kbf0", poolH[:, 2 * U: 2 * U + 10240], m_qkv_ev[-1:])
                tap_sbuf("vbf0", poolH[:, 2 * (U + 5120): 2 * (U + 5120) + 10240],
                         m_qkv_ev[-1:])

            m_sim = [None] * 4
            m_maskread = [None] * 4
            m_att = [None] * 4
            m_outpe = [None] * 4
            m_relu = [None] * 4

            def emit_sims(j):
                bk = PA[j % 2]
                S.wait("tensor", bank_free.get(bk.name))
                S.waits("tensor", m_qkv_ev)
                last = None
                for dc in range(KC):
                    mm(bk.ap()[:, 0:128], KBF[dc][:, j * 128:j * 128 + 128],
                       QBF[dc][:, j * 128:(j + 1) * 128],
                       start=(dc == 0), stop=(dc == KC - 1))
                for dc in range(KC):
                    last = mm(bk.ap()[:, 128:256],
                              KBF[dc][:, 128 + j * 128:128 + j * 128 + 128],
                              QBF[dc][:, j * 128:(j + 1) * 128],
                              start=(dc == 0), stop=(dc == KC - 1),
                              sem="s_pe" if dc == KC - 1 else None)
                m_sim[j] = last

            def emit_attn_ew(j):
                bk = PA[j % 2]
                mprev = mview(0) if (c == 0 and j == 0) else mview(1)
                S.wait("vector", m_small)
                S.wait("vector", m_sim[j])
                if j >= 1:
                    S.wait("vector", m_relu[j - 1])
                else:
                    S.wait("vector", attn_guard["relu3"])
                S.op("vector", lambda e, b=bk, m=mprev: e.tensor_tensor(
                    S1[:, 0:128], b.ap()[:, 0:128], m, ALU.mult))
                m_mr = S.op("vector", lambda e, b=bk: e.tensor_tensor(
                    S1[:, 128:256], b.ap()[:, 128:256], mview(2), ALU.mult),
                    sem="s_dve")
                m_maskread[j] = m_mr
                bank_free[bk.name] = m_mr
                S.wait("scalar", m_mr)
                if j >= 1:
                    S.wait("scalar", m_att[j - 1])
                else:
                    S.wait("scalar", attn_guard["att3"])
                m_rl = S.op("scalar", lambda e: e.activation(
                    S2[:, 0:256], S1[:, 0:256], AF.Relu, scale=1.0 / 128.0),
                    sem="s_act")
                m_relu[j] = m_rl
                S.wait("vector", m_rl)
                if j >= 2:
                    S.wait("vector", m_outpe[j - 2])
                else:
                    S.wait("vector", attn_guard["out2"] if j == 0 else attn_guard["out3"])
                m_sq = S.op("vector", lambda e, j=j: e.tensor_tensor(
                    ATT[j % 2][:, 0:256], S2[:, 0:256], S2[:, 0:256], ALU.mult),
                    sem="s_dve")
                m_att[j] = m_sq

            def emit_outs(j):
                S.wait("tensor", m_att[j])
                last = None
                for dc in range(KC):
                    bk = PM[dc % 2]
                    S.wait("tensor", bank_free.get(bk.name))
                    mm(bk.ap()[:, 0:128], VBF[j][:, dc * 128:(dc + 1) * 128],
                       ATT[j % 2][:, 0:128], start=True, stop=False)
                    last = mm(bk.ap()[:, 0:128], VBF[j + 1][:, dc * 128:(dc + 1) * 128],
                              ATT[j % 2][:, 128:256], start=False, stop=True,
                              sem="s_pe")
                    S.wait("scalar", last)
                    m_e = S.op("scalar", lambda e, dc=dc, j=j, b=bk: e.activation(
                        ARAW[dc][:, j * 128:(j + 1) * 128], b.ap()[:, 0:128], AF.Copy),
                        sem="s_act")
                    bank_free[bk.name] = m_e
                m_outpe[j] = last

            emit_sims(0)
            emit_attn_ew(0)
            emit_sims(1)
            emit_attn_ew(1)
            emit_outs(0)
            emit_sims(2)
            emit_attn_ew(2)
            emit_outs(1)
            emit_sims(3)
            emit_attn_ew(3)
            emit_outs(2)
            emit_outs(3)
            attn_guard["relu3"] = m_relu[3]
            attn_guard["att3"] = m_att[3]
            attn_guard["out2"] = m_outpe[2]
            attn_guard["out3"] = m_outpe[3]
            m_att_act = S.mark("s_act")
            if c == 0:
                tap_sbuf("araw0", poolH[:, 2 * (U + 10240): 2 * (U + 10240) + 8192],
                         [m_att_act])

            m_r1 = []
            for fg in range(4):
                slot = fg % 2
                S.wait("sync", m_wcp[12 + fg])
                m_slab = slab_load(slot, wbf.ap()[:, 3 * D + fg * 512:
                                                  3 * D + (fg + 1) * 512])
                S.wait("tensor", m_slab)
                S.wait("tensor", m_att_act)
                for fcg in range(4):
                    fc = fg * 4 + fcg
                    bk = PMPA[(fg * 4 + fcg) % 6]
                    S.wait("tensor", bank_free.get(bk.name))
                    m_g = None
                    for kc in range(KC):
                        m_g = mm(bk.ap()[:, 0:512],
                                 slab_tile(slot, kc, fcg * 128, 128),
                                 ARAW[kc][:, 0:512],
                                 start=(kc == 0), stop=(kc == KC - 1),
                                 sem="s_pe" if kc == KC - 1 else None)
                    S.wait("vector", m_g)
                    S.wait("vector", m_x_ld)
                    m_add = S.op("vector", lambda e, fc=fc, b=bk: e.tensor_tensor(
                        XR[fc][:, 128:640], b.ap()[:, 0:512], XR[fc][:, 128:640],
                        ALU.add), sem="s_dve")
                    bank_free[bk.name] = m_add
                    m_r1.append(m_add)
                slab_cons[slot] = S.mark("s_pe")
            r1dst = r1s.ap()[:, c * 512:(c + 1) * 512] \
                .rearrange("(fc p) t -> p fc t", p=128)
            r1src = poolF[:, O_XR:O_XR + 10240] \
                .rearrange("p (fc t) -> p fc t", t=640)[:, :, 128:640]
            S.wait("sync", m_r1[7])
            S.dma("sync", r1dst[:, 0:8], r1src[:, 0:8], "d_st")
            S.wait("sync", m_r1[15])
            m_r1_store[c] = S.dma("sync", r1dst[:, 8:16], r1src[:, 8:16], "d_st")
            prev_attn_pe = S.mark("s_pe")

        m_aend = S.mark("s_pe")
        emit_pfold()
        m_b_gate = [m_aend, pf_state["dve"]] + pf_state["st"]
        tap_dram("r1", r1s.ap(), [m_r1_store[NCH - 1]])

        S.wait("gpsimd", ("s_cc", 1))
        m_bt = []
        for l in range(2):
            m_bt.append(S.dma(
                "gpsimd", col_btot.ap()[:, l * 16:(l + 1) * 16],
                fold_out.ap()[l * FOLD_LROWS + 2048, :].rearrange("(a p) -> p a", p=128),
                "d_bt", allow_slow_non_contiguous=True))
        S.waits("gpsimd", m_bt)
        S.wait("gpsimd", m_small)
        m_btot = S.op("gpsimd", lambda e: e.tensor_tensor(
            col_btot.ap(), col_entpb.ap(), col_btot.ap(), ALU.add), sem="s_pool")
        tap_sbuf("btot", col_btot.ap(), [m_btot])

        m_out_store = None
        for c in range(NCH):
            S.wait("sync", m_out_store)
            S.wait("sync", m_r1_store[NCH - 1])
            r1src = r1s.ap()[:, c * 512:(c + 1) * 512] \
                .rearrange("(fc p) t -> p fc t", p=128)
            xdst = poolF[:, O_XR:O_XR + 10240] \
                .rearrange("p (fc t) -> p fc t", t=640)[:, :, 128:640]
            m_r1ld_a = S.dma("sync", xdst[:, 0:8], r1src[:, 0:8], "d_x")
            m_r1ld = S.dma("sync", xdst[:, 8:16], r1src[:, 8:16], "d_x")

            r1_fcms = [m_r1ld_a] * 8 + [m_r1ld] * 8
            m_ln2, m_ln2_fc = emit_ln(XR, LNBF, None, [(128, 512)], None, None,
                                      pre_ms=[m_aend], pre_fc=r1_fcms)
            if c == 0:
                tap_sbuf("lnr10", poolH[:, 2 * O_LNBF: 2 * O_LNBF + 10240], [m_ln2])

            m_amp_ev = [None] * KC
            m_ph_ev = [None] * KC
            for fg in range(8):
                slot = fg % 2
                wbase = 4 * D if fg < 4 else 5 * D
                S.wait("sync", m_wcp[(16 if fg < 4 else 20) + (fg % 4)])
                m_slab = slab_load(slot, wbf.ap()[:, wbase + (fg % 4) * 512:
                                                  wbase + (fg % 4 + 1) * 512])
                S.wait("tensor", m_slab)
                for fcg in range(4):
                    fc = (fg % 4) * 4 + fcg
                    bk = PMPA[(fg * 4 + fcg) % 6]
                    S.wait("tensor", bank_free.get(bk.name))
                    m_g = None
                    for kc in range(KC):
                        S.wait("tensor", m_ln2_fc[kc])
                        m_g = mm(bk.ap()[:, 0:512],
                                 slab_tile(slot, kc, fcg * 128, 128),
                                 LNBF[kc][:, 128:640],
                                 start=(kc == 0), stop=(kc == KC - 1),
                                 sem="s_pe" if kc == KC - 1 else None)
                    S.wait("scalar", m_g)
                    S.waits("scalar", m_b_gate + [m_ampbh, m_small])
                    if fg < 4:
                        m_e = S.op("scalar", lambda e, fc=fc, b=bk: e.activation(
                            HA[fc][:, :], b.ap()[:, 0:512], AF.Tanh, scale=0.5,
                            bias=col_ampb.ap()[:, fc:fc + 1]), sem="s_act")
                        m_amp_ev[fc] = m_e
                    else:
                        m_e = S.op("scalar", lambda e, fc=fc, b=bk: e.activation(
                            HB[fc][:, :], b.ap()[:, 0:512], AF.Tanh,
                            bias=col_phb.ap()[:, fc:fc + 1]), sem="s_act")
                        m_ph_ev[fc] = m_e
                    bank_free[bk.name] = m_e
                slab_cons[slot] = S.mark("s_pe")

            m_chain_prev = None
            m_cos_prev = None
            m_cs_prev = None
            m_h0cast = []
            for fc in range(KC):
                S.wait("vector", m_ph_ev[fc])
                S.wait("vector", m_cos_prev)
                S.op("vector", lambda e, fc=fc: e.tensor_scalar(
                    TS[2][:, :], HB[fc][:, :], 0.5, None, ALU.is_gt))
                m_wr = S.op("vector", lambda e, fc=fc: e.scalar_tensor_tensor(
                    TS[2][:, :], TS[2][:, :], col_m2.ap(), HB[fc][:, :],
                    ALU.mult, ALU.add), sem="s_dve")
                S.wait("scalar", m_cs_prev)
                m_s = S.op("scalar", lambda e, fc=fc: e.activation(
                    TS[3][:, :], HB[fc][:, :], AF.Sin, scale=PI), sem="s_act")
                S.wait("scalar", m_wr)
                S.wait("scalar", m_sets)
                m_cos = S.op("scalar", lambda e: e.activation(
                    TS[4][:, :], TS[2][:, :], AF.Sin, scale=PI,
                    bias=pihalf_col.ap()), sem="s_act")
                m_cos_prev = m_cos
                m_av = S.op("scalar", lambda e, fc=fc: e.activation(
                    TS[2][:, :], HA[fc][:, :], AF.Copy, scale=0.5,
                    bias=0.5), sem="s_act")
                S.wait("vector", m_cos)
                m_cs = S.op("vector", lambda e: e.tensor_tensor(
                    TS[5][:, :], TS[3][:, :], TS[4][:, :], ALU.add), sem="s_dve")
                m_cs_prev = m_cs
                S.wait("vector", m_av)
                m_h0 = S.op("vector", lambda e, fc=fc: e.tensor_tensor(
                    HA[fc][:, :], TS[5][:, :], TS[2][:, :], ALU.mult), sem="s_dve")
                m_chain_prev = m_h0
                S.wait("scalar", m_h0)
                m_cc = S.op("scalar", lambda e, fc=fc: e.activation(
                    QBF[fc][:, :], HA[fc][:, :], AF.Copy), sem="s_act")
                m_h0cast.append(m_cc)
            if c == 0:
                tms = tap_sbuf("h0", poolF[:, U:U + 8192], [m_chain_prev])
                if tms:
                    m_h0cast = m_h0cast + [tms]

            hcur, hnxt = HA, HB
            m_hbf = list(m_h0cast[:KC])
            m_hcur = [m_chain_prev]
            for l in range(2):
                m_pre = []
                for fg in range(4):
                    slot = fg % 2
                    S.wait("sync", ("s_cc", 1))
                    m_slab = slab_load(
                        slot,
                        fold_out.ap()[l * FOLD_LROWS: l * FOLD_LROWS + 2048,
                                      fg * 512:(fg + 1) * 512])
                    S.wait("tensor", m_slab)
                    for fcg in range(4):
                        fc = fg * 4 + fcg
                        bk = PMPA[(fg * 4 + fcg) % 6]
                        S.wait("tensor", bank_free.get(bk.name))
                        m_g = None
                        for kc in range(KC):
                            S.wait("tensor", m_hbf[kc])
                            m_g = mm(bk.ap()[:, 0:512],
                                     slab_tile(slot, kc, fcg * 128, 128),
                                     QBF[kc][:, :],
                                     start=(kc == 0), stop=(kc == KC - 1),
                                     sem="s_pe" if kc == KC - 1 else None)
                        S.wait("vector", m_g)
                        S.wait("vector", m_btot)
                        S.waits("vector", m_hcur)
                        m_p = S.op("vector", lambda e, fc=fc, l=l, b=bk, hc=hcur, hn=hnxt:
                                   e.scalar_tensor_tensor(
                                       hn[fc][:, :], b.ap()[:, 0:512],
                                       col_btot.ap()[:, l * 16 + fc:l * 16 + fc + 1],
                                       hc[fc][:, :], ALU.add, ALU.add), sem="s_dve")
                        bank_free[bk.name] = m_p
                        m_pre.append(m_p)
                    slab_cons[slot] = S.mark("s_pe")
                if c == 0 and l == 0:
                    tms = tap_sbuf("pre3", poolF[:, U + 8192:U + 16384], m_pre[-1:])
                    if tms:
                        m_pre = m_pre + [tms]
                hcur, hnxt = hnxt, hcur
                m_ln3, m_ln3_fc = emit_ln(hcur, QBF, hnxt, [(0, 512)],
                                          col_g3.ap()[:, l * 16:(l + 1) * 16],
                                          col_b3.ap()[:, l * 16:(l + 1) * 16],
                                          pre_ms=m_pre)
                hcur, hnxt = hnxt, hcur
                m_hbf = list(m_ln3_fc)
                m_hcur = [m_ln3]
                if c == 0 and l == 0:
                    tms = tap_sbuf("h1", poolF[:, U:U + 8192], [m_ln3])
                    if tms:
                        m_hbf = m_hbf + [tms]
                        m_hcur = m_hcur + [tms]
                if c == 0 and l == 1:
                    tms = tap_sbuf("pre3b", poolF[:, U + 8192:U + 16384], m_pre[-1:])
                    if tms:
                        m_pre = m_pre + [tms]
                if c == 0 and l == 0:
                    tq = tap_sbuf("qbfa", poolH[:, 2 * O_QBF: 2 * O_QBF + 8192], m_hbf)
                    if tq:
                        m_hbf = m_hbf + [tq]
                m_pre4 = []
                for fg in range(4):
                    slot = fg % 2
                    pq = "sync"
                    S.wait(pq, ("s_cc", 2))
                    S.wait(pq, slab_cons[slot])
                    m_slab = None
                    for h in range(2):
                        for rr in range(8):
                            row0 = rr * 512 + l * 256 + h * 128
                            src = bass.AP(pg_out, row0 * D + fg * 512,
                                          [[D, 128], [1, 512]])
                            m_slab = S.dma(pq, slab_tile(slot, h * 8 + rr, 0, 512),
                                           src, f"d_sl{slot}s")
                    S.wait("tensor", m_slab)
                    for fcg in range(4):
                        fc = fg * 4 + fcg
                        bk = PMPA[(fg * 4 + fcg) % 6]
                        S.wait("tensor", bank_free.get(bk.name))
                        m_g = None
                        for kc in range(KC):
                            pos = (kc % 2) * 8 + kc // 2
                            S.wait("tensor", m_hbf[kc])
                            m_g = mm(bk.ap()[:, 0:512],
                                     slab_tile(slot, pos, fcg * 128, 128),
                                     QBF[kc][:, :],
                                     start=(kc == 0), stop=(kc == KC - 1),
                                     sem="s_pe" if kc == KC - 1 else None)
                        S.wait("vector", m_g)
                        S.waits("vector", m_hcur)
                        m_p = S.op("vector", lambda e, fc=fc, l=l, b=bk, hc=hcur, hn=hnxt:
                                   e.scalar_tensor_tensor(
                                       hn[fc][:, :], b.ap()[:, 0:512],
                                       col_ptld.ap()[:, l * 16 + fc:l * 16 + fc + 1],
                                       hc[fc][:, :], ALU.add, ALU.add), sem="s_dve")
                        bank_free[bk.name] = m_p
                        m_pre4.append(m_p)
                    slab_cons[slot] = S.mark("s_pe")
                if c == 0 and l == 0:
                    tq2 = tap_sbuf("qbfb", poolH[:, 2 * O_QBF: 2 * O_QBF + 8192],
                                   m_pre4[-1:])
                    if tq2:
                        m_pre4 = m_pre4 + [tq2]
                hcur, hnxt = hnxt, hcur
                m_ln4, m_ln4_fc = emit_ln(hcur, QBF, hnxt, [(0, 512)],
                                          col_g4.ap()[:, l * 16:(l + 1) * 16],
                                          col_b4.ap()[:, l * 16:(l + 1) * 16],
                                          pre_ms=m_pre4)
                hcur, hnxt = hnxt, hcur
                m_hbf = list(m_ln4_fc)
                m_hcur = [m_ln4]
                if c == 0 and l == 0:
                    t1 = tap_sbuf("pre4", poolF[:, U + 8192:U + 16384], m_pre4[-1:])
                    t2 = tap_sbuf("h2", poolF[:, U:U + 8192], [m_ln4])
                    for t in (t1, t2):
                        if t:
                            m_hbf = m_hbf + [t]
                            m_hcur = m_hcur + [t]
                if c == 0 and l == 1:
                    t3 = tap_sbuf("h4", poolF[:, U:U + 8192], [m_ln4])
                    if t3:
                        m_hcur = m_hcur + [t3]

            m_fin = []
            for fc in range(KC):
                S.wait("vector", m_ln4_fc[fc])
                S.waits("vector", m_hcur[1:])
                S.wait("vector", m_r1ld)
                m_f = S.op("vector", lambda e, fc=fc, hc=hcur: e.tensor_tensor(
                    XR[fc][:, 128:640], hc[fc][:, :], XR[fc][:, 128:640], ALU.add),
                    sem="s_dve")
                m_fin.append(m_f)
            odst = out_fm.ap()[:, c * 512:(c + 1) * 512] \
                .rearrange("(fc p) t -> p fc t", p=128)
            osrc = poolF[:, O_XR:O_XR + 10240] \
                .rearrange("p (fc t) -> p fc t", t=640)[:, :, 128:640]
            S.wait("sync", m_fin[7])
            S.dma("sync", odst[:, 0:8], osrc[:, 0:8], "d_st")
            S.wait("sync", m_fin[15])
            m_out_store = S.dma("sync", odst[:, 8:16], osrc[:, 8:16], "d_st")

        S.wait("sync", S.mark("d_st"))
        S.wait("sync", S.mark("d_tap"))

        with nc.Block() as blk:
            @blk.sync
            def _(eng):
                for fn in S.q["sync"]:
                    fn(eng)

            @blk.gpsimd
            def _(eng):
                for fn in S.q["gpsimd"]:
                    fn(eng)

            @blk.vector
            def _(eng):
                for fn in S.q["vector"]:
                    fn(eng)

            @blk.scalar
            def _(eng):
                for fn in S.q["scalar"]:
                    fn(eng)

            @blk.tensor
            def _(eng):
                for fn in S.q["tensor"]:
                    fn(eng)

    return nc


_BUILD_CACHE = {}


def _host_inputs(inputs):
    x = np.asarray(inputs["L0_attn_x"], dtype=np.float32)
    w_attn = np.asarray(inputs["L0_attn_w_attn"], dtype=np.float32)
    w_o = np.asarray(inputs["L0_attn_w_o"], dtype=np.float32)
    amp_w = np.asarray(inputs["L0_ffn_amp_w"], dtype=np.float32)
    amp_b = np.asarray(inputs["L0_ffn_amp_b"], dtype=np.float32)
    ph_w = np.asarray(inputs["L0_ffn_ph_w"], dtype=np.float32)
    ph_b = np.asarray(inputs["L0_ffn_ph_b"], dtype=np.float32)
    ent_w = np.asarray(inputs["L0_ffn_ent_w"], dtype=np.float32)
    ent_b = np.asarray(inputs["L0_ffn_ent_b"], dtype=np.float32)
    ent_pw = np.asarray(inputs["L0_ffn_ent_pw"], dtype=np.float32)
    ent_pb = np.asarray(inputs["L0_ffn_ent_pb"], dtype=np.float32)
    path_w = np.asarray(inputs["L0_ffn_path_w"], dtype=np.float32)
    path_b = np.asarray(inputs["L0_ffn_path_b"], dtype=np.float32)
    phase = np.asarray(inputs["L0_ffn_phase"], dtype=np.float32)
    ln3_g = np.asarray(inputs["L0_ffn_ln3_g"], dtype=np.float32)
    ln3_b = np.asarray(inputs["L0_ffn_ln3_b"], dtype=np.float32)
    ln4_g = np.asarray(inputs["L0_ffn_ln4_g"], dtype=np.float32)
    ln4_b = np.asarray(inputs["L0_ffn_ln4_b"], dtype=np.float32)

    toks = np.ascontiguousarray(x.reshape(2 * 8192, D))

    def colize(v, n):
        return np.ascontiguousarray(v.reshape(n, 128).T)

    shared = {
        "w_qkvT": np.ascontiguousarray(w_attn.T),
        "w_oT": np.ascontiguousarray(w_o.T),
        "w_amp": np.ascontiguousarray(amp_w),
        "w_ph": np.ascontiguousarray(ph_w),
        "entpb_col": np.ascontiguousarray(
            np.stack([colize(ent_pb[l], 16) for l in range(2)])),
        "pathb_col": np.ascontiguousarray(
            np.stack([[colize(path_b[l, p], 16) for p in range(2)] for l in range(2)])),
        "phase_col": np.ascontiguousarray(
            np.stack([[colize(phase[l, p], 16) for p in range(2)] for l in range(2)])),
        "ampb_col": colize(amp_b, 16),
        "phb_col": colize(ph_b, 16),
        "ln3_gcol": np.ascontiguousarray(np.stack([colize(ln3_g[l], 16)
                                                   for l in range(2)])),
        "ln4_gcol": np.ascontiguousarray(np.stack([colize(ln4_g[l], 16)
                                                   for l in range(2)])),
        "ln3_bcol": np.ascontiguousarray(np.stack([colize(ln3_b[l], 16)
                                                   for l in range(2)])),
        "ln4_bcol": np.ascontiguousarray(np.stack([colize(ln4_b[l], 16)
                                                   for l in range(2)])),
    }
    sidx = np.arange(128)
    m_prev = (sidx[:, None] >= sidx[None, :]).astype(np.float32)
    m_cur = (sidx[:, None] <= sidx[None, :]).astype(np.float32)
    entwT = np.ascontiguousarray(np.transpose(ent_w, (0, 2, 1)))

    in_maps = []
    for c in range(NCORES):
        r0 = c * TPC
        xs = np.zeros((HALO + TPC, D), dtype=np.float32)
        if c in (0, 4):
            xs[HALO:] = toks[r0:r0 + TPC]
            m0 = np.zeros((128, 128), dtype=np.float32)
        else:
            xs[:] = toks[r0 - HALO:r0 + TPC]
            m0 = m_prev
        j0 = c * JSL
        d0 = c * 256
        im = dict(shared)
        im["x_fm"] = np.ascontiguousarray(xs.T)
        im["entwT_sl"] = np.ascontiguousarray(entwT[:, j0:j0 + JSL, :])
        im["entpw_sl"] = np.ascontiguousarray(ent_pw[:, j0:j0 + JSL, :])
        im["entb_col"] = np.ascontiguousarray(
            np.stack([colize(ent_b[l, j0:j0 + JSL], 8) for l in range(2)]))
        im["pathw_sl"] = np.ascontiguousarray(path_w[:, :, d0:d0 + 256, :])
        im["masks"] = np.ascontiguousarray(np.stack([m0, m_prev, m_cur]))
        in_maps.append(im)
    return in_maps


def kernel(**inputs):
    in_maps = _host_inputs(inputs)
    key = "main"
    if key not in _BUILD_CACHE:
        _BUILD_CACHE[key] = build()
    nc = _BUILD_CACHE[key]
    res = run_bass_kernel_spmd(nc, in_maps, core_ids=list(range(NCORES)))
    out = np.empty((2 * 8192, D), dtype=np.float32)
    for c in range(NCORES):
        out[c * TPC:(c + 1) * TPC] = np.asarray(res.results[c]["out_fm"]).T
    return out.reshape(2, 8192, D)
